# version 02
# baseline (speedup 1.0000x reference)
import contextlib
import numpy as np

import concourse.bass as bass
import concourse.mybir as mybir
from concourse.bass_utils import run_bass_kernel_spmd

F32 = mybir.dt.float32
BF16 = mybir.dt.bfloat16
I32 = mybir.dt.int32
AF = mybir.ActivationFunctionType
ALU = mybir.AluOpType
AXX = mybir.AxisListType.X

NCORES = 8
NT, D = 2048, 1024
TT = NT // 128
DC = D // 128
EPS = 1e-5

ENGINES = ("tensor", "vector", "scalar", "gpsimd", "sync")


class Sem:
    def __init__(self, handle):
        self.h = handle
        self.n = 0

    def bump(self, amount=1):
        self.n += amount
        return self.n


class Prog:
    def __init__(self, nc):
        self.nc = nc
        self.ops = {e: [] for e in ENGINES}
        self.es = contextlib.ExitStack()
        self.sems = {}
        self.sb_bytes = 0
        self.bar = self.sem("BAR")
        self.esem = {e: self.sem("E_" + e)
                     for e in ("tensor", "vector", "scalar", "gpsimd")}

    def sem(self, name):
        if name not in self.sems:
            self.sems[name] = Sem(self.es.enter_context(self.nc.semaphore(name)))
        return self.sems[name]

    def sbuf(self, name, shape, dtype=F32):
        t = self.es.enter_context(self.nc.sbuf_tensor(name, list(shape), dtype))
        n = 1
        for sz in shape[1:]:
            n *= sz
        self.sb_bytes += n * (2 if dtype == BF16 else 4)
        return t

    _name_n = 0

    def sbuf_in(self, stack, name, shape, dtype=F32):
        Prog._name_n += 1
        t = stack.enter_context(
            self.nc.sbuf_tensor(f"{name}_{Prog._name_n}", list(shape), dtype))
        return t

    _psum_n = 0

    def psum(self, shape=(128, 512)):
        Prog._psum_n += 1
        return self.es.enter_context(
            self.nc.psum_tensor(f"ps{Prog._psum_n}", list(shape), F32))

    def on(self, eng, fn):
        self.ops[eng].append(fn)

    def _emit(self, eng, fn, chain):
        sem = self.esem[eng]
        prev = sem.n

        def wrapped(e, fn=fn, prev=prev, chain=chain, h=sem.h):
            if chain and prev > 0:
                e.wait_ge(h, prev)
            fn(e).then_inc(h, 1)
        self.ops[eng].append(wrapped)
        sem.bump()
        return sem.n

    def t(self, fn, chain=False):
        return self._emit("tensor", fn, chain)

    def v(self, fn, chain=True):
        return self._emit("vector", fn, chain)

    def s(self, fn, chain=True):
        return self._emit("scalar", fn, chain)

    def g(self, fn, chain=True):
        return self._emit("gpsimd", fn, chain)

    def wait(self, eng, sem, val=None):
        v = sem.n if val is None else val
        if v > 0:
            self.on(eng, lambda e, s=sem.h, v=v: e.wait_ge(s, v))

    def dma(self, dst, src, sem=None, q="sync"):
        sm = sem if sem is not None else self.sem("sDMA")
        self.on(q, lambda e, a=dst, b=src, s=sm.h: e.dma_start(out=a, in_=b).then_inc(s, 16))
        sm.bump(16)
        return sm

    def barrier(self, engines=ENGINES):
        b = self.bar
        for e in engines:
            self.on(e, lambda eng, s=b.h: eng.sem_inc(s, 1))
        b.n += len(engines)
        v = b.n
        for e in engines:
            self.on(e, lambda eng, s=b.h, v=v: eng.wait_ge(s, v))

    def emit(self, block):
        for e in ENGINES:
            lst = self.ops[e]

            def run(eng, lst=lst):
                for f in lst:
                    f(eng)
            getattr(block, e)(run)


class Banks:

    def __init__(self, p, ids=range(8)):
        self.p = p
        self.ids = list(ids)
        self.banks = [p.psum() for _ in self.ids]
        self.freed = [None] * len(self.ids)
        self.rr = 0
        self.reserved = set()

    def take(self):
        for _ in range(len(self.ids)):
            i = self.rr % len(self.ids)
            self.rr += 1
            if i in self.reserved:
                continue
            if self.freed[i] is not None:
                sem, val = self.freed[i]
                self.p.wait("tensor", sem, val)
            return i, self.banks[i]
        raise RuntimeError("all banks reserved")

    def done(self, i, eng):
        sem = self.p.esem[eng]
        self.freed[i] = (sem, sem.n)

    def at(self, i):
        if self.freed[i] is not None:
            sem, val = self.freed[i]
            self.p.wait("tensor", sem, val)
        return self.banks[i]



def _row(v):
    return np.ascontiguousarray(np.asarray(v).reshape(1, -1).astype(np.float32))


def build_host_inputs(inp, core):
    d = {}
    d["x"] = np.ascontiguousarray(inp["L0_attn_a_x"][core])
    d["c_ident"] = np.eye(128, dtype=np.float32)
    d["c_ones"] = np.ones((128, 512), dtype=np.float32)
    cc = np.arange(896)[None, :] - 384
    pp = np.arange(128)[:, None]
    d["c_mpad"] = (cc >= pp).astype(np.float32)
    m4 = np.where(np.arange(128)[None, :] >= np.arange(128)[:, None], 0.0, -1e30).astype(np.float32)
    d["c_minf4"] = np.ascontiguousarray(np.tile(m4, (1, 4)))
    bd = np.zeros((128, 128), dtype=np.float32)
    bd[:64, :64] = 1.0
    bd[64:, 64:] = 1.0
    d["c_blkdiag"] = bd
    d["c_scal"] = np.tile(np.array(CONST_VALS, np.float32), (128, 1))

    for k in ("Wq", "Wv", "Wo"):
        d[f"aa_{k}"] = np.ascontiguousarray(inp[f"L0_attn_a_{k}"].astype(np.float32))
    for k in ("bq", "bv", "bo"):
        d[f"aa_{k}"] = _row(inp[f"L0_attn_a_{k}"])

    for i in range(1, 5):
        d[f"fn_cw{i}"] = np.ascontiguousarray(inp[f"L0_ffn_cn_w{i}"].astype(np.float32))
        d[f"fn_cb{i}"] = _row(inp[f"L0_ffn_cn_b{i}"])
    for i in range(1, 4):
        d[f"fn_gw{i}"] = np.ascontiguousarray(inp[f"L0_ffn_gc_w{i}"].astype(np.float32))
        d[f"fn_gb{i}"] = _row(inp[f"L0_ffn_gc_b{i}"])
    d["fn_ar"] = _row(inp["L0_ffn_adaptive_range"])

    d["mb_inwT"] = np.ascontiguousarray(inp["L0_attn_b_in_w"].astype(np.float32).T)
    d["mb_convw"] = np.ascontiguousarray(
        inp["L0_attn_b_conv_w"].astype(np.float32).reshape(18, 128, 4)
        .transpose(1, 0, 2).reshape(128, 72))
    d["mb_convb"] = np.ascontiguousarray(
        inp["L0_attn_b_conv_b"].astype(np.float32).reshape(18, 128).T)
    d["mb_dtb"] = np.ascontiguousarray(inp["L0_attn_b_dt_bias"].astype(np.float32).reshape(32, 1))
    d["mb_alog"] = np.ascontiguousarray(inp["L0_attn_b_A_log"].astype(np.float32).reshape(32, 1))
    d["mb_dp"] = np.ascontiguousarray(np.tile(inp["L0_attn_b_Dp"].astype(np.float32), (128, 1)))
    rep = np.zeros((32, 2048), np.float32)
    for h in range(32):
        rep[h, h * 64:(h + 1) * 64] = 1.0
    d["mb_rep64"] = rep
    d["mb_nwrep"] = np.ascontiguousarray(np.tile(inp["L0_attn_b_norm_w"].astype(np.float32), (128, 1)))
    d["mb_outwT"] = np.ascontiguousarray(inp["L0_attn_b_out_w"].astype(np.float32).T)

    d["la_wqkv"] = np.ascontiguousarray(inp["L1_attn_a_w_qkv"].astype(np.float32))
    d["la_wg"] = np.ascontiguousarray(inp["L1_attn_a_w_gates"].astype(np.float32))
    d["la_wo"] = np.ascontiguousarray(inp["L1_attn_a_w_out"].astype(np.float32))

    d["lf_w1T"] = np.ascontiguousarray(inp["L1_ffn_w1"].astype(np.float32).T)
    d["lf_w2T"] = np.ascontiguousarray(inp["L1_ffn_w2"].astype(np.float32).T)
    cw = inp["L1_ffn_conv_w"].astype(np.float32)[:, 0, :]
    d["lf_cw"] = np.ascontiguousarray(
        cw.reshape(32, 128, 3).transpose(1, 2, 0).reshape(128, 96))
    d["lf_cb"] = np.ascontiguousarray(
        inp["L1_ffn_conv_b"].astype(np.float32).reshape(32, 128).T)

    d["lb_wproj"] = np.ascontiguousarray(inp["L1_attn_b_Wproj"].astype(np.float32))
    d["lb_bproj"] = _row(inp["L1_attn_b_bproj"])
    d["lb_w0d"] = np.ascontiguousarray(
        np.vstack([inp["L1_attn_b_W0"], inp["L1_attn_b_W0"]]).astype(np.float32))
    lng = inp["L1_attn_b_ln_g"].astype(np.float32)
    lnb = inp["L1_attn_b_ln_b"].astype(np.float32)
    d["lb_lngrep"] = np.ascontiguousarray(np.tile(np.concatenate([lng, lng]), (128, 1)))
    d["lb_lnbrep"] = np.ascontiguousarray(np.tile(np.concatenate([lnb, lnb]), (128, 1)))
    gg = np.empty((2048,), np.float32)
    gg[0::2] = inp["L1_attn_b_gn_g"].astype(np.float32)
    gg[1::2] = inp["L1_attn_b_gn_b"].astype(np.float32)
    d["lb_gngbrep"] = np.ascontiguousarray(np.tile(gg, (128, 1)))
    d["lb_wout"] = np.ascontiguousarray(inp["L1_attn_b_Wout"].astype(np.float32))
    d["lb_bout"] = _row(inp["L1_attn_b_bout"])
    return d


DBG_PHASE = None
CONST_VALS = [1e-5, 50.0, -50.0, 0.5, -1.0, 2.0, 0.125 / 50.0, 1.0, -0.5]



def build_program(n_sub=8):
    nc = bass.Bass("TRN2", target_bir_lowering=False, detect_race_conditions=False)
    dr = {}

    def din(name, shape, dtype=F32):
        dr[name] = nc.dram_tensor(name, list(shape), dtype, kind="ExternalInput").ap()
        return dr[name]

    def dint(name, shape, dtype=F32):
        dr[name] = nc.dram_tensor(name, list(shape), dtype, kind="Internal").ap()
        return dr[name]

    din("x", (NT, D))
    din("c_ident", (128, 128))
    din("c_ones", (128, 512))
    din("c_mpad", (128, 896))
    din("c_blkdiag", (128, 128))
    din("c_scal", (128, len(CONST_VALS)))
    din("c_minf4", (128, 512))
    for k in ("Wq", "Wv", "Wo"):
        din(f"aa_{k}", (D, D))
    for k in ("bq", "bv", "bo"):
        din(f"aa_{k}", (1, D))
    for i, (a, b) in enumerate([(D, D), (D, 512), (512, 256), (256, 1)], 1):
        din(f"fn_cw{i}", (a, b))
        din(f"fn_cb{i}", (1, b))
    for i, (a, b) in enumerate([(D, 512), (512, 256), (256, 1)], 1):
        din(f"fn_gw{i}", (a, b))
        din(f"fn_gb{i}", (1, b))
    din("fn_ar", (1, 1))
    din("mb_inwT", (D, 4384))
    din("mb_convw", (128, 72))
    din("mb_convb", (128, 18))
    din("mb_dtb", (32, 1))
    din("mb_alog", (32, 1))
    din("mb_dp", (128, 32))
    din("mb_rep64", (32, 2048))
    din("mb_nwrep", (128, 2048))
    din("mb_outwT", (2048, D))
    dint("ydr", (NT, 2048))
    din("la_wqkv", (D, 3072))
    din("la_wg", (D, 16))
    din("la_wo", (D, D))
    dint("attdr", (D, NT))
    din("lf_w1T", (D, 4096))
    din("lf_w2T", (4096, D))
    din("lf_cw", (128, 96))
    din("lf_cb", (128, 32))
    dint("w1qdr", (D, 4096), BF16)
    dint("w2qdr", (4096, D), BF16)
    dint("h2dr", (4096, NT))
    dint("xq2dr", (4096, NT), BF16)
    din("lb_wproj", (D, 3072))
    din("lb_bproj", (1, 3072))
    din("lb_w0d", (128, 64))
    din("lb_lngrep", (128, 128))
    din("lb_lnbrep", (128, 128))
    din("lb_gngbrep", (128, 2 * D))
    din("lb_wout", (D, D))
    din("lb_bout", (1, D))

    out_d = nc.dram_tensor("out", [NT, D], F32, kind="ExternalOutput").ap()
    dbgx_d = nc.dram_tensor("dbgx", [128, DC * NT], F32, kind="ExternalOutput").ap() \
        if DBG_PHASE else None

    xd = [dr["x"]] + [dint(f"xd{i}", (NT, D)) for i in range(1, n_sub)] + [out_d]

    p = Prog(nc)
    eT, eV, eS, eG = (p.esem[e] for e in ("tensor", "vector", "scalar", "gpsimd"))
    B = Banks(p, range(6))
    PL = [p.psum(), p.psum()]

    XLT = p.sbuf("XLT", (128, DC, NT))
    IDT = p.sbuf("IDT", (128, 128))
    ONES = p.sbuf("ONES", (128, 512))
    MPAD = p.sbuf("MPAD", (128, 896))
    BLKD = p.sbuf("BLKD", (128, 128))
    XT = p.sbuf("XT", (128, 2, D))
    SC = p.sbuf("SC", (128, D))
    MINF = p.sbuf("MINF", (128, 512))
    LNC = p.sbuf("LNC", (128, 48))
    CONSTS = p.sbuf("CONSTS", (128, len(CONST_VALS)))
    for ci, cv in enumerate(CONST_VALS):
        nc.const_aps.register(F32, float(cv), CONSTS[:, ci:ci + 1])

    sDMA = p.sem("sDMA")
    sXD = p.sem("sXD")
    sST = p.sem("sST")
    sDBG = p.sem("sDBG")

    p.dma(IDT[:], dr["c_ident"][:])
    p.dma(ONES[:], dr["c_ones"][:])
    p.dma(MPAD[:], dr["c_mpad"][:])
    p.dma(BLKD[:], dr["c_blkdiag"][:])
    p.dma(CONSTS[:], dr["c_scal"][:])
    p.dma(MINF[:], dr["c_minf4"][:])
    for e in ("vector", "scalar", "tensor", "gpsimd"):
        p.wait(e, sDMA)

    def rstd_newton(v_ap, out_ap, t1, t2):
        p.wait("scalar", eV)
        p.s(lambda e: e.activation(out=t1, in_=v_ap, func=AF.Ln))
        p.s(lambda e: e.activation(out=t2, in_=t1, func=AF.Exp, scale=-0.5))
        p.wait("vector", eS)
        p.v(lambda e: e.tensor_tensor(out=t1, in0=t2, in1=t2, op=ALU.mult))
        p.v(lambda e: e.tensor_tensor(out=t1, in0=t1, in1=v_ap, op=ALU.mult))
        p.v(lambda e: e.tensor_scalar(out=t1, in0=t1, scalar1=-0.5, scalar2=1.5,
                                      op0=ALU.mult, op1=ALU.add))
        p.v(lambda e: e.tensor_tensor(out=out_ap, in0=t2, in1=t1, op=ALU.mult))

    def dbg_dump_xlt(tag):
        if DBG_PHASE != tag:
            return
        p.barrier()
        for j in range(DC):
            p.dma(dbgx_d[:, j * NT:(j + 1) * NT], XLT[:, j, :], sem=sDBG)
        p.wait("sync", sDBG)
        p.barrier()

    def xflat(off, dims):
        st = XLT.ap().ap[0][0]
        return bass.AP(XLT.ap().tensor, off, [[st, 128]] + dims)

    def ln_into_xlt(src, tok=False):
        for i in range(TT):
            p.dma(XT[:, 0, :], src[i * 128:(i + 1) * 128, :], sem=sXD)
            p.wait("vector", sXD)
            p.wait("vector", eT)
            p.v(lambda e: e.tensor_tensor(out=SC[:, :], in0=XT[:, 0, :],
                                          in1=XT[:, 0, :], op=ALU.mult))
            p.v(lambda e: e.tensor_reduce(out=LNC[:, 32:33], in_=SC[:, :],
                                          axis=AXX, op=ALU.add))
            p.v(lambda e: e.tensor_reduce(out=LNC[:, 0:1], in_=XT[:, 0, :],
                                          axis=AXX, op=ALU.add))
            p.v(lambda e: e.tensor_scalar(out=LNC[:, 0:1], in0=LNC[:, 0:1],
                                          scalar1=1.0 / D, scalar2=None, op0=ALU.mult))
            p.v(lambda e: e.tensor_scalar(out=LNC[:, 32:33], in0=LNC[:, 32:33],
                                          scalar1=1.0 / D, scalar2=None, op0=ALU.mult))
            p.v(lambda e: e.tensor_tensor(out=LNC[:, 33:34], in0=LNC[:, 0:1],
                                          in1=LNC[:, 0:1], op=ALU.mult))
            p.v(lambda e: e.tensor_tensor(out=LNC[:, 32:33], in0=LNC[:, 32:33],
                                          in1=LNC[:, 33:34], op=ALU.subtract))
            p.v(lambda e: e.tensor_scalar(out=LNC[:, 32:33], in0=LNC[:, 32:33],
                                          scalar1=EPS, scalar2=None, op0=ALU.add))
            rstd_newton(LNC[:, 32:33], LNC[:, 16:17], LNC[:, 34:35], LNC[:, 35:36])
            p.v(lambda e: e.tensor_scalar(out=SC[:, :], in0=XT[:, 0, :],
                                          scalar1=LNC[:, 0:1], scalar2=LNC[:, 16:17],
                                          op0=ALU.subtract, op1=ALU.mult))
            if tok:
                p.wait("scalar", eV)
                p.s(lambda e, i=i: e.copy(out=xflat(i * D, [[1, D]]), in_=SC[:, :]))
            else:
                p.wait("tensor", eV)
                for j in range(DC):
                    bi, bk = B.take()
                    p.t(lambda e, j=j, bk=bk: e.transpose(
                        bk[:, 0:128], SC[:, j * 128:(j + 1) * 128], IDT[:]))
                    if j % 2 == 0:
                        p.wait("scalar", eT)
                        p.s(lambda e, i=i, j=j, bk=bk: e.copy(
                            out=XLT[:, j, i * 128:(i + 1) * 128], in_=bk[:, 0:128]))
                        B.done(bi, "scalar")
                    else:
                        p.wait("vector", eT)
                        p.v(lambda e, i=i, j=j, bk=bk: e.tensor_copy(
                            out=XLT[:, j, i * 128:(i + 1) * 128], in_=bk[:, 0:128]))
                        B.done(bi, "vector")
            p.wait("sync", eV)
            p.wait("sync", eS)
        p.barrier()

    def l0_attn_a(src, dst):
        ls = contextlib.ExitStack()
        s_scale = float((64 ** 0.5) ** (-0.25))
        CTX = p.sbuf_in(ls, "aa_CTX", (128, 8, 128))
        QS = p.sbuf_in(ls, "aa_QS", (128, D))
        VS = p.sbuf_in(ls, "aa_VS", (128, D))
        EQ = p.sbuf_in(ls, "aa_EQ", (128, D))
        SMC = p.sbuf_in(ls, "aa_SMC", (128, 48))
        CSUM = p.sbuf_in(ls, "aa_CSUM", (128, 16))
        CS = p.sbuf_in(ls, "aa_CS", (1, D))
        BR = p.sbuf_in(ls, "aa_BR", (1, 3 * D))
        W0B = p.sbuf_in(ls, "aa_W0B", (128, 8192))
        W1B = p.sbuf_in(ls, "aa_W1B", (128, 8192))

        for kk in range(DC):
            p.dma(W0B[:, kk * D:(kk + 1) * D], dr["aa_Wq"][kk * 128:(kk + 1) * 128, :])
            p.dma(W1B[:, kk * D:(kk + 1) * D], dr["aa_Wv"][kk * 128:(kk + 1) * 128, :])
        p.dma(BR[:, 0:D], dr["aa_bq"][:])
        p.dma(BR[:, D:2 * D], dr["aa_bv"][:])
        p.dma(BR[:, 2 * D:3 * D], dr["aa_bo"][:])
        p.wait("tensor", sDMA)
        p.wait("vector", sDMA)

        p.v(lambda e: e.memset(CTX[:, :, :], 0.0))
        p.wait("tensor", eV)

        q3 = QS.ap().rearrange("p (h d) -> p h d", d=64)
        smc_max_b = bass.AP(SMC.ap().tensor, 0, [[SMC.ap().ap[0][0], 128], [1, 16], [0, 64]])
        smc_rec_b = bass.AP(SMC.ap().tensor, 16, [[SMC.ap().ap[0][0], 128], [1, 16], [0, 64]])

        for i in range(TT):
            for half, (wb, boff, dst_t) in enumerate([(W0B, 0, QS), (W1B, D, VS)]):
                for eb in range(2):
                    bi, bk = B.take()
                    for kk in range(DC):
                        p.t(lambda e, kk=kk, i=i, eb=eb, bk=bk, wb=wb:
                            e.matmul(bk[:, 0:512], XLT[:, kk, i * 128:(i + 1) * 128],
                                     wb[:, kk * D + eb * 512: kk * D + eb * 512 + 512],
                                     start=(kk == 0), stop=False))
                    p.t(lambda e, eb=eb, bk=bk, boff=boff:
                        e.matmul(bk[:, 0:512], ONES[0:1, 0:128],
                                 BR[0:1, boff + eb * 512: boff + eb * 512 + 512],
                                 start=False, stop=True))
                    if half == 0:
                        p.wait("vector", eT)
                        p.v(lambda e, eb=eb, bk=bk, dst_t=dst_t: e.tensor_copy(
                            out=dst_t[:, eb * 512:(eb + 1) * 512], in_=bk[:, 0:512]))
                        B.done(bi, "vector")
                    else:
                        p.wait("scalar", eT)
                        p.s(lambda e, eb=eb, bk=bk, dst_t=dst_t: e.copy(
                            out=dst_t[:, eb * 512:(eb + 1) * 512], in_=bk[:, 0:512]))
                        B.done(bi, "scalar")
            p.v(lambda e: e.tensor_reduce(out=SMC[:, 0:16], in_=q3, axis=AXX, op=ALU.max))
            p.v(lambda e: e.tensor_tensor(out=q3, in0=q3, in1=smc_max_b, op=ALU.subtract))
            p.wait("scalar", eV)
            p.s(lambda e: e.activation(out=QS[:, :], in_=QS[:, :], func=AF.Exp))
            p.wait("vector", eS)
            p.v(lambda e: e.tensor_reduce(out=SMC[:, 32:48], in_=q3, axis=AXX, op=ALU.add))
            p.v(lambda e: e.reciprocal(out=SMC[:, 16:32], in_=SMC[:, 32:48]))
            p.v(lambda e: e.tensor_scalar(out=SMC[:, 16:32], in0=SMC[:, 16:32],
                                          scalar1=s_scale, scalar2=None, op0=ALU.mult))
            p.v(lambda e: e.tensor_tensor(out=q3, in0=q3, in1=smc_rec_b, op=ALU.mult))
            p.wait("scalar", eV)
            p.wait("scalar", eT)
            p.s(lambda e: e.activation(out=EQ[:, :], in_=QS[:, :], func=AF.Exp))
            p.wait("tensor", eS)
            for half in range(2):
                p.t(lambda e, half=half, i=i: e.matmul(
                    PL[half][0:1, 0:512], ONES[:, 0:1], EQ[:, half * 512:(half + 1) * 512],
                    start=(i == 0), stop=(i == TT - 1)))
            for pp in range(8):
                bi, bk = B.take()
                p.t(lambda e, pp=pp, bk=bk: e.matmul(
                    bk[:, 0:128], EQ[:, pp * 128:(pp + 1) * 128],
                    VS[:, pp * 128:(pp + 1) * 128], start=True, stop=True))
                p.wait("vector", eT)
                p.v(lambda e, pp=pp, bk=bk: e.tensor_tensor(
                    out=CTX[:, pp, :], in0=CTX[:, pp, :], in1=bk[:, 0:128], op=ALU.add))
                B.done(bi, "vector")
            p.wait("tensor", eV)
            for j in range(DC):
                bi, bk = B.take()
                p.t(lambda e, j=j, bk=bk: e.transpose(
                    bk[:, 0:128], QS[:, j * 128:(j + 1) * 128], IDT[:]))
                if j % 2 == 0:
                    p.wait("scalar", eT)
                    p.s(lambda e, i=i, j=j, bk=bk: e.copy(
                        out=XLT[:, j, i * 128:(i + 1) * 128], in_=bk[:, 0:128]))
                    B.done(bi, "scalar")
                else:
                    p.wait("vector", eT)
                    p.v(lambda e, i=i, j=j, bk=bk: e.tensor_copy(
                        out=XLT[:, j, i * 128:(i + 1) * 128], in_=bk[:, 0:128]))
                    B.done(bi, "vector")
            p.wait("vector", eT)
            p.wait("scalar", eT)
        p.barrier()

        dbg_dump_xlt("q1")
        for half in range(2):
            p.v(lambda e, half=half: e.tensor_copy(
                out=CS[0:1, half * 512:(half + 1) * 512], in_=PL[half][0:1, 0:512]))
        p.wait("tensor", eV)
        for j in range(8):
            bi, bk = B.take()
            p.t(lambda e, j=j, bk=bk: e.matmul(
                bk[0:128, 0:1], CS[0:1, j * 128:(j + 1) * 128], ONES[0:1, 0:1],
                start=True, stop=True))
            p.wait("vector", eT)
            p.v(lambda e, j=j, bk=bk: e.tensor_copy(
                out=CSUM[:, j:j + 1], in_=bk[0:128, 0:1]))
            B.done(bi, "vector")
        p.v(lambda e: e.reciprocal(out=CSUM[:, 8:16], in_=CSUM[:, 0:8]))
        p.v(lambda e: e.tensor_scalar(out=CSUM[:, 8:16], in0=CSUM[:, 8:16],
                                      scalar1=s_scale, scalar2=None, op0=ALU.mult))
        p.wait("gpsimd", eV)
        for pp in range(8):
            p.g(lambda e, pp=pp: e.tensor_scalar(
                out=CTX[:, pp, :], in0=CTX[:, pp, :],
                scalar1=CSUM[:, 8 + pp:9 + pp], scalar2=None, op0=ALU.mult))
            p.g(lambda e, pp=pp: e.tensor_tensor(
                out=CTX[:, pp, :], in0=CTX[:, pp, :], in1=BLKD[:], op=ALU.mult))
        p.barrier()

        if DBG_PHASE == "ctx":
            p.barrier()
            for pp in range(8):
                p.dma(dbgx_d[:, pp * 128:(pp + 1) * 128], CTX[:, pp, :], sem=sDBG)
            p.dma(dbgx_d[:, 1024:1040], CSUM[:, :], sem=sDBG)
            p.wait("sync", sDBG)
            p.barrier()

        for pp in range(8):
            for tb in range(4):
                bi, bk = B.take()
                p.t(lambda e, pp=pp, tb=tb, bk=bk: e.matmul(
                    bk[:, 0:512], CTX[:, pp, :], XLT[:, pp, tb * 512:(tb + 1) * 512],
                    start=True, stop=True))
                p.wait("scalar", eT)
                p.s(lambda e, pp=pp, tb=tb, bk=bk: e.copy(
                    out=XLT[:, pp, tb * 512:(tb + 1) * 512], in_=bk[:, 0:512]))
                B.done(bi, "scalar")
                p.wait("tensor", eS)
        p.barrier()

        dbg_dump_xlt("attn")
        for kk in range(DC):
            p.dma(W0B[:, kk * D:(kk + 1) * D], dr["aa_Wo"][kk * 128:(kk + 1) * 128, :])
        p.wait("tensor", sDMA)
        for i in range(TT):
            p.wait("sync", sST)
            p.dma(XT[:, 1, :], src[i * 128:(i + 1) * 128, :], sem=sXD)
            p.wait("vector", sXD)
            for fb in range(2):
                bi, bk = B.take()
                for kk in range(DC):
                    p.t(lambda e, kk=kk, i=i, fb=fb, bk=bk: e.matmul(
                        bk[:, 0:512], XLT[:, kk, i * 128:(i + 1) * 128],
                        W0B[:, kk * D + fb * 512: kk * D + fb * 512 + 512],
                        start=(kk == 0), stop=False))
                p.t(lambda e, fb=fb, bk=bk: e.matmul(
                    bk[:, 0:512], ONES[0:1, 0:128],
                    BR[0:1, 2 * D + fb * 512: 2 * D + fb * 512 + 512],
                    start=False, stop=True))
                p.wait("vector", eT)
                p.v(lambda e, fb=fb, bk=bk: e.tensor_tensor(
                    out=XT[:, 1, fb * 512:(fb + 1) * 512],
                    in0=XT[:, 1, fb * 512:(fb + 1) * 512],
                    in1=bk[:, 0:512], op=ALU.add))
                B.done(bi, "vector")
            p.wait("sync", eV)
            p.dma(dst[i * 128:(i + 1) * 128, :], XT[:, 1, :], sem=sST)
        p.barrier()
        ls.close()


    def l0_ffn(src, dst):
        ls = contextlib.ExitStack()
        OT = p.sbuf_in(ls, "fn_OT", (128, 8, 512))
        H1 = p.sbuf_in(ls, "fn_H1", (128, 8, 512))
        H2 = p.sbuf_in(ls, "fn_H2", (128, 4, 512))
        H3 = p.sbuf_in(ls, "fn_H3", (128, 2, 512))
        TFR = XT[0:1, 0, 0:1024]
        COUPT = p.sbuf_in(ls, "fn_COUPT", (128, 8))
        GCT = p.sbuf_in(ls, "fn_GCT", (128, 8))
        G1 = p.sbuf_in(ls, "fn_G1", (128, 4))
        G2 = p.sbuf_in(ls, "fn_G2", (128, 2))
        GFC = p.sbuf_in(ls, "fn_GFC", (1, 8))
        ARt = p.sbuf_in(ls, "fn_AR", (1, 1))
        S1 = p.sbuf_in(ls, "fn_S1", (128, 256))
        S2 = p.sbuf_in(ls, "fn_S2", (128, 256))
        CSg = SC[0:1, 0:1024]
        W0B = p.sbuf_in(ls, "fn_W0B", (128, 8192))
        W1B = p.sbuf_in(ls, "fn_W1B", (128, 8192))
        FBR = W1B

        fb = 5122
        cb1o, cb2o, cb3o, cb4o = fb + 0, fb + 1024, fb + 1536, fb + 1792
        gb1o, gb2o, gb3o = fb + 1800, fb + 2312, fb + 2568

        def out_tile(i):
            return xflat(i * D, [[1, D]])

        for kk in range(DC):
            p.dma(W1B[:, kk * 512:(kk + 1) * 512],
                  dr["fn_gw1"][kk * 128:(kk + 1) * 128, :])
        for kk in range(4):
            p.dma(W1B[:, 4096 + kk * 256:4096 + (kk + 1) * 256],
                  dr["fn_gw2"][kk * 128:(kk + 1) * 128, :])
        for kk in range(2):
            p.dma(W1B[:, 5120 + kk:5121 + kk],
                  dr["fn_gw3"][kk * 128:(kk + 1) * 128, :])
        p.dma(FBR[0:1, cb1o:cb1o + 1024], dr["fn_cb1"][:])
        p.dma(FBR[0:1, cb2o:cb2o + 512], dr["fn_cb2"][:])
        p.dma(FBR[0:1, cb3o:cb3o + 256], dr["fn_cb3"][:])
        p.dma(FBR[0:1, cb4o:cb4o + 1], dr["fn_cb4"][:])
        p.dma(FBR[0:1, gb1o:gb1o + 512], dr["fn_gb1"][:])
        p.dma(FBR[0:1, gb2o:gb2o + 256], dr["fn_gb2"][:])
        p.dma(FBR[0:1, gb3o:gb3o + 1], dr["fn_gb3"][:])
        p.dma(ARt[:], dr["fn_ar"][:])
        p.wait("tensor", sDMA)
        p.wait("vector", sDMA)
        p.wait("scalar", sDMA)

        for i in range(TT):
            for half in range(2):
                p.t(lambda e, i=i, half=half: e.matmul(
                    PL[half][0:1, 0:512], ONES[:, 0:1],
                    xflat(i * D + half * 512, [[1, 512]]),
                    start=(i == 0), stop=(i == TT - 1)))
        p.wait("vector", eT)
        for half in range(2):
            p.v(lambda e, half=half: e.tensor_scalar(
                out=CSg[0:1, half * 512:(half + 1) * 512], in0=PL[half][0:1, 0:512],
                scalar1=1.0 / NT, scalar2=None, op0=ALU.mult))
        p.wait("tensor", eV)
        for j in range(8):
            bi, bk = B.take()
            p.t(lambda e, j=j, bk=bk: e.matmul(
                bk[0:128, 0:1], CSg[0:1, j * 128:(j + 1) * 128], ONES[0:1, 0:1],
                start=True, stop=True))
            p.wait("vector", eT)
            p.v(lambda e, j=j, bk=bk: e.tensor_copy(out=GCT[:, j:j + 1], in_=bk[0:128, 0:1]))
            B.done(bi, "vector")
        p.wait("tensor", eV)
        for ec in range(4):
            bi, bk = B.take()
            for kk in range(DC):
                p.t(lambda e, ec=ec, kk=kk, bk=bk: e.matmul(
                    bk[0:128, 0:1], W1B[:, kk * 512 + ec * 128: kk * 512 + ec * 128 + 128],
                    GCT[:, kk:kk + 1], start=(kk == 0), stop=False))
            p.t(lambda e, ec=ec, bk=bk: e.matmul(
                bk[0:128, 0:1], FBR[0:1, gb1o + ec * 128: gb1o + ec * 128 + 128],
                ONES[0:1, 0:1], start=False, stop=True))
            p.wait("scalar", eT)
            p.s(lambda e, ec=ec, bk=bk: e.activation(
                out=G1[:, ec:ec + 1], in_=bk[0:128, 0:1], func=AF.Gelu))
            B.done(bi, "scalar")
        p.wait("tensor", eS)
        for ec in range(2):
            bi, bk = B.take()
            for kk in range(4):
                p.t(lambda e, ec=ec, kk=kk, bk=bk: e.matmul(
                    bk[0:128, 0:1], W1B[:, 4096 + kk * 256 + ec * 128: 4096 + kk * 256 + ec * 128 + 128],
                    G1[:, kk:kk + 1], start=(kk == 0), stop=False))
            p.t(lambda e, ec=ec, bk=bk: e.matmul(
                bk[0:128, 0:1], FBR[0:1, gb2o + ec * 128: gb2o + ec * 128 + 128],
                ONES[0:1, 0:1], start=False, stop=True))
            p.wait("scalar", eT)
            p.s(lambda e, ec=ec, bk=bk: e.activation(
                out=G2[:, ec:ec + 1], in_=bk[0:128, 0:1], func=AF.Gelu))
            B.done(bi, "scalar")
        p.wait("tensor", eS)
        bi, bk = B.take()
        for kk in range(2):
            p.t(lambda e, kk=kk, bk=bk: e.matmul(
                bk[0:1, 0:1], W1B[:, 5120 + kk:5121 + kk], G2[:, kk:kk + 1],
                start=(kk == 0), stop=False))
        p.t(lambda e, bk=bk: e.matmul(
            bk[0:1, 0:1], FBR[0:1, gb3o:gb3o + 1], ONES[0:1, 0:1],
            start=False, stop=True))
        p.wait("scalar", eT)
        p.s(lambda e, bk=bk: e.activation(out=GFC[0:1, 0:1], in_=bk[0:1, 0:1],
                                          func=AF.Sigmoid))
        B.done(bi, "scalar")
        p.wait("vector", eS)
        p.v(lambda e: e.tensor_scalar(out=GFC[0:1, 1:2], in0=ARt[0:1, 0:1],
                                      scalar1=0.6, scalar2=None, op0=ALU.mult))
        p.v(lambda e: e.tensor_tensor(out=GFC[0:1, 3:4], in0=ARt[0:1, 0:1],
                                      in1=GFC[0:1, 0:1], op=ALU.mult))
        p.v(lambda e: e.tensor_scalar(out=GFC[0:1, 3:4], in0=GFC[0:1, 3:4],
                                      scalar1=1.4, scalar2=0.1, op0=ALU.mult, op1=ALU.add))
        p.v(lambda e: e.tensor_tensor(out=GFC[0:1, 2:3], in0=GFC[0:1, 3:4],
                                      in1=ARt[0:1, 0:1], op=ALU.subtract))

        p.wait("sync", eT)
        for kk in range(DC):
            p.dma(W0B[:, kk * D:(kk + 1) * D], dr["fn_cw1"][kk * 128:(kk + 1) * 128, :])
        for kk in range(DC):
            p.dma(W1B[:, kk * 512:(kk + 1) * 512], dr["fn_cw2"][kk * 128:(kk + 1) * 128, :])
        for kk in range(4):
            p.dma(W1B[:, 4096 + kk * 256:4096 + (kk + 1) * 256],
                  dr["fn_cw3"][kk * 128:(kk + 1) * 128, :])
        for kk in range(2):
            p.dma(W1B[:, 5120 + kk:5121 + kk], dr["fn_cw4"][kk * 128:(kk + 1) * 128, :])
        p.wait("tensor", sDMA)

        for cyc in range(3):
            for tb in range(4):
                p.wait("tensor", eV)
                for ii in range(4):
                    ti = tb * 4 + ii
                    for j in range(DC):
                        bi, bk = B.take()
                        p.t(lambda e, ti=ti, j=j, bk=bk: e.transpose(
                            bk[:, 0:128],
                            xflat(ti * D + j * 128, [[1, 128]]), IDT[:]))
                        if j % 2 == 0:
                            p.wait("scalar", eT)
                            p.s(lambda e, ii=ii, j=j, bk=bk: e.copy(
                                out=OT[:, j, ii * 128:(ii + 1) * 128], in_=bk[:, 0:128]))
                            B.done(bi, "scalar")
                        else:
                            p.wait("vector", eT)
                            p.v(lambda e, ii=ii, j=j, bk=bk: e.tensor_copy(
                                out=OT[:, j, ii * 128:(ii + 1) * 128], in_=bk[:, 0:128]))
                            B.done(bi, "vector")
                p.wait("tensor", eV)
                p.wait("tensor", eS)
                for ec in range(8):
                    bi, bk = B.take()
                    for kk in range(DC):
                        p.t(lambda e, ec=ec, kk=kk, bk=bk: e.matmul(
                            bk[:, 0:512], W0B[:, kk * D + ec * 128: kk * D + ec * 128 + 128],
                            OT[:, kk, :], start=(kk == 0), stop=False))
                    p.t(lambda e, ec=ec, bk=bk: e.matmul(
                        bk[:, 0:512], FBR[0:1, cb1o + ec * 128: cb1o + ec * 128 + 128],
                        ONES[0:1, 0:512], start=False, stop=True))
                    p.wait("scalar", eT)
                    p.s(lambda e, ec=ec, bk=bk: e.activation(
                        out=H1[:, ec, :], in_=bk[:, 0:512], func=AF.Gelu))
                    B.done(bi, "scalar")
                p.wait("tensor", eS)
                for ec in range(4):
                    bi, bk = B.take()
                    for kk in range(DC):
                        p.t(lambda e, ec=ec, kk=kk, bk=bk: e.matmul(
                            bk[:, 0:512], W1B[:, kk * 512 + ec * 128: kk * 512 + ec * 128 + 128],
                            H1[:, kk, :], start=(kk == 0), stop=False))
                    p.t(lambda e, ec=ec, bk=bk: e.matmul(
                        bk[:, 0:512], FBR[0:1, cb2o + ec * 128: cb2o + ec * 128 + 128],
                        ONES[0:1, 0:512], start=False, stop=True))
                    p.wait("scalar", eT)
                    p.s(lambda e, ec=ec, bk=bk: e.activation(
                        out=H2[:, ec, :], in_=bk[:, 0:512], func=AF.Gelu))
                    B.done(bi, "scalar")
                p.wait("tensor", eS)
                for ec in range(2):
                    bi, bk = B.take()
                    for kk in range(4):
                        p.t(lambda e, ec=ec, kk=kk, bk=bk: e.matmul(
                            bk[:, 0:512], W1B[:, 4096 + kk * 256 + ec * 128: 4096 + kk * 256 + ec * 128 + 128],
                            H2[:, kk, :], start=(kk == 0), stop=False))
                    p.t(lambda e, ec=ec, bk=bk: e.matmul(
                        bk[:, 0:512], FBR[0:1, cb3o + ec * 128: cb3o + ec * 128 + 128],
                        ONES[0:1, 0:512], start=False, stop=True))
                    p.wait("scalar", eT)
                    p.s(lambda e, ec=ec, bk=bk: e.activation(
                        out=H3[:, ec, :], in_=bk[:, 0:512], func=AF.Gelu))
                    B.done(bi, "scalar")
                p.wait("tensor", eS)
                bi, bk = B.take()
                for kk in range(2):
                    p.t(lambda e, kk=kk, bk=bk: e.matmul(
                        bk[0:1, 0:512], W1B[:, 5120 + kk:5121 + kk], H3[:, kk, :],
                        start=(kk == 0), stop=False))
                p.t(lambda e, bk=bk: e.matmul(
                    bk[0:1, 0:512], FBR[0:1, cb4o:cb4o + 1], ONES[0:1, 0:512],
                    start=False, stop=True))
                p.wait("scalar", eT)
                p.s(lambda e, bk=bk: e.activation(
                    out=TFR[0:1, 0:512], in_=bk[0:1, 0:512], func=AF.Sigmoid))
                B.done(bi, "scalar")
                p.wait("vector", eS)
                p.v(lambda e: e.tensor_scalar(
                    out=TFR[0:1, 512:1024], in0=TFR[0:1, 0:512],
                    scalar1=GFC[0:1, 1:2], scalar2=GFC[0:1, 2:3],
                    op0=ALU.mult, op1=ALU.add))
                p.wait("tensor", eV)
                for ii in range(4):
                    bi, bk = B.take()
                    p.t(lambda e, ii=ii, bk=bk: e.matmul(
                        bk[0:128, 0:1], TFR[0:1, 512 + ii * 128: 512 + (ii + 1) * 128],
                        ONES[0:1, 0:1], start=True, stop=True))
                    p.wait("vector", eT)
                    p.v(lambda e, ii=ii, bk=bk: e.tensor_copy(
                        out=COUPT[:, ii:ii + 1], in_=bk[0:128, 0:1]))
                    B.done(bi, "vector")
                p.v(lambda e: e.tensor_scalar(
                    out=COUPT[:, 4:8], in0=COUPT[:, 0:4],
                    scalar1=-1.0, scalar2=None, op0=ALU.mult))
                for ii in range(4):
                    ti = tb * 4 + ii
                    q = [out_tile(ti)[:, 0:256], out_tile(ti)[:, 256:512],
                         out_tile(ti)[:, 512:768], out_tile(ti)[:, 768:1024]]
                    c = COUPT[:, ii:ii + 1]
                    nc_ = COUPT[:, 4 + ii:5 + ii]
                    p.v(lambda e, q=q, c=c: e.scalar_tensor_tensor(
                        out=S1[:, :], in0=q[2], scalar=c, in1=q[0],
                        op0=ALU.mult, op1=ALU.add))
                    p.v(lambda e, q=q, nc_=nc_: e.scalar_tensor_tensor(
                        out=q[2], in0=q[0], scalar=nc_, in1=q[2],
                        op0=ALU.mult, op1=ALU.add))
                    p.v(lambda e, q=q: e.tensor_copy(out=q[0], in_=S1[:, :]))
                    p.v(lambda e, q=q, nc_=nc_: e.scalar_tensor_tensor(
                        out=S2[:, :], in0=q[3], scalar=nc_, in1=q[1],
                        op0=ALU.mult, op1=ALU.add))
                    p.v(lambda e, q=q, c=c: e.scalar_tensor_tensor(
                        out=q[3], in0=q[1], scalar=c, in1=q[3],
                        op0=ALU.mult, op1=ALU.add))
                    p.v(lambda e, q=q: e.tensor_copy(out=q[1], in_=S2[:, :]))
        p.barrier()

        for i in range(TT):
            p.wait("sync", sST)
            p.dma(XT[:, 1, :], src[i * 128:(i + 1) * 128, :], sem=sXD)
            p.wait("vector", sXD)
            p.v(lambda e, i=i: e.tensor_tensor(
                out=XT[:, 1, :], in0=XT[:, 1, :], in1=out_tile(i), op=ALU.add))
            p.wait("sync", eV)
            p.dma(dst[i * 128:(i + 1) * 128, :], XT[:, 1, :], sem=sST)
        p.barrier()
        ls.close()


    def l0_attn_b(src, dst):
        ls = contextlib.ExitStack()
        WS = p.sbuf_in(ls, "mb_WS", (128, 18, 256))
        RAW = p.sbuf_in(ls, "mb_RAW", (128, 264))
        CARR = p.sbuf_in(ls, "mb_CARR", (128, 72))
        CW = p.sbuf_in(ls, "mb_CW", (128, 72))
        CB = p.sbuf_in(ls, "mb_CB", (128, 18))
        XTOK = p.sbuf_in(ls, "mb_XTOK", (128, 2048))
        XDF = p.sbuf_in(ls, "mb_XDF", (128, 2048))
        YTL = p.sbuf_in(ls, "mb_YTL", (128, 2048))
        ST = p.sbuf_in(ls, "mb_ST", (128, 2048))
        GT = p.sbuf_in(ls, "mb_GT", (128, 128))
        BTOK = p.sbuf_in(ls, "mb_BTOK", (128, 128))
        DTW = p.sbuf_in(ls, "mb_DTW", (128, 256))
        DTR = p.sbuf_in(ls, "mb_DTR", (32, 256))
        DTT = p.sbuf_in(ls, "mb_DTT", (32, 256))
        DSC = p.sbuf_in(ls, "mb_DSC", (32, 256))
        ACU = p.sbuf_in(ls, "mb_ACU", (32, 256))
        ODW = p.sbuf_in(ls, "mb_ODW", (32, 256))
        DDS = p.sbuf_in(ls, "mb_DDS", (32, 128))
        TMJ = p.sbuf_in(ls, "mb_TMJ", (128, 128))
        MT0 = p.sbuf_in(ls, "mb_MT0", (128, 128))
        MT1 = p.sbuf_in(ls, "mb_MT1", (128, 128))
        XDS = p.sbuf_in(ls, "mb_XDS", (128, 2048))
        MCOL = p.sbuf_in(ls, "mb_MCOL", (32, 8))
        DPB = p.sbuf_in(ls, "mb_DPB", (128, 32))
        MXW = p.sbuf_in(ls, "mb_MXW", (128, 4096))

        p.dma(CW[:], dr["mb_convw"][:])
        p.dma(CB[:], dr["mb_convb"][:])
        p.dma(MCOL[:, 0:1], dr["mb_dtb"][:])
        p.dma(MCOL[:, 1:2], dr["mb_alog"][:])
        p.dma(DPB[:], dr["mb_dp"][:])
        for kk in range(DC):
            p.dma(DTW[:, kk * 32:(kk + 1) * 32],
                  dr["mb_inwT"][kk * 128:(kk + 1) * 128, 4352:4384])
        p.wait("scalar", sDMA)
        p.wait("vector", sDMA)
        p.wait("tensor", sDMA)
        p.s(lambda e: e.activation(out=MCOL[:, 2:3], in_=MCOL[:, 1:2], func=AF.Exp))
        p.wait("vector", eS)
        p.v(lambda e: e.tensor_scalar(out=MCOL[:, 2:3], in0=MCOL[:, 2:3],
                                      scalar1=-1.0, scalar2=None, op0=ALU.mult))
        p.v(lambda e: e.memset(ST[:, :], 0.0))
        p.v(lambda e: e.memset(CARR[:, :], 0.0))
        p.wait("scalar", eV)
        p.wait("tensor", eV)

        sWX = p.sem("sWX")
        for win in range(8):
            w0 = win * 256
            p.wait("vector", eT)
            bi, bk = B.take()
            for kk in range(DC):
                p.t(lambda e, kk=kk, w0=w0, bk=bk: e.matmul(
                    bk[0:32, 0:256], DTW[:, kk * 32:(kk + 1) * 32],
                    XLT[:, kk, w0:w0 + 256], start=(kk == 0), stop=(kk == DC - 1)))
            p.wait("scalar", eT)
            p.s(lambda e, bk=bk: e.copy(out=DTR[:, :], in_=bk[0:32, 0:256]))
            B.done(bi, "scalar")
            p.wait("vector", eS)
            p.v(lambda e: e.tensor_scalar(out=DSC[:, :], in0=DTR[:, :],
                                          scalar1=MCOL[:, 0:1], scalar2=None,
                                          op0=ALU.add))
            p.v(lambda e: e.scalar_tensor_tensor(out=DSC[:, :], in0=DSC[:, :],
                                                 scalar=-1.0, in1=DSC[:, :],
                                                 op0=ALU.mult, op1=ALU.max))
            p.v(lambda e: e.tensor_scalar(out=DTT[:, :], in0=DTR[:, :],
                                          scalar1=MCOL[:, 0:1], scalar2=0.0,
                                          op0=ALU.add, op1=ALU.max))
            p.wait("scalar", eV)
            p.s(lambda e: e.activation(out=DSC[:, :], in_=DSC[:, :], func=AF.Exp,
                                       scale=-1.0))
            p.s(lambda e: e.activation(out=DSC[:, :], in_=DSC[:, :], func=AF.Ln,
                                       bias=1.0))
            p.wait("vector", eS)
            p.v(lambda e: e.tensor_tensor(out=DTT[:, :], in0=DTT[:, :], in1=DSC[:, :],
                                          op=ALU.add))
            p.v(lambda e: e.tensor_scalar(out=DSC[:, :], in0=DTT[:, :],
                                          scalar1=MCOL[:, 2:3], scalar2=None, op0=ALU.mult))
            for c in range(2):
                p.v(lambda e, c=c: e.tensor_tensor_scan(
                    out=ACU[:, c * 128:(c + 1) * 128], data0=DSC[:, c * 128:(c + 1) * 128],
                    data1=DSC[:, c * 128:(c + 1) * 128], initial=0.0,
                    op0=ALU.add, op1=ALU.bypass))
            p.wait("scalar", eV)
            p.s(lambda e: e.activation(out=ODW[:, :], in_=ACU[:, :], func=AF.Exp))

            for ec in range(18):
                slot = ec % 4
                if True:
                    p.wait("sync", eT)
                    for kk in range(DC):
                        p.dma(MXW[:, slot * 1024 + kk * 128: slot * 1024 + (kk + 1) * 128],
                              dr["mb_inwT"][kk * 128:(kk + 1) * 128,
                                            2048 + ec * 128: 2048 + (ec + 1) * 128],
                              sem=sWX)
                p.wait("tensor", sWX)
                bi, bk = B.take()
                for kk in range(DC):
                    p.t(lambda e, kk=kk, slot=slot, w0=w0, bk=bk: e.matmul(
                        bk[:, 0:256], MXW[:, slot * 1024 + kk * 128: slot * 1024 + kk * 128 + 128],
                        XLT[:, kk, w0:w0 + 256], start=(kk == 0), stop=(kk == DC - 1)))
                p.wait("vector", eT)
                p.v(lambda e, ec=ec, bk=bk: e.tensor_copy(out=RAW[:, 4:260], in_=bk[:, 0:256]))
                B.done(bi, "vector")
                p.v(lambda e, ec=ec: e.tensor_copy(out=RAW[:, 0:4], in_=CARR[:, ec * 4:(ec + 1) * 4]))
                p.v(lambda e, ec=ec: e.tensor_copy(out=CARR[:, ec * 4:(ec + 1) * 4], in_=RAW[:, 256:260]))
                p.v(lambda e, ec=ec: e.tensor_scalar(
                    out=SC[:, 0:256], in0=RAW[:, 4:260], scalar1=CW[:, ec * 4 + 3:ec * 4 + 4],
                    scalar2=CB[:, ec:ec + 1], op0=ALU.mult, op1=ALU.add))
                for kq, sh in ((2, 1), (1, 2), (0, 3)):
                    p.v(lambda e, ec=ec, kq=kq, sh=sh: e.scalar_tensor_tensor(
                        out=SC[:, 0:256], in0=RAW[:, 4 - sh:260 - sh],
                        scalar=CW[:, ec * 4 + kq:ec * 4 + kq + 1],
                        in1=SC[:, 0:256], op0=ALU.mult, op1=ALU.add))
                p.wait("scalar", eV)
                p.s(lambda e, ec=ec: e.activation(out=WS[:, ec, :], in_=SC[:, 0:256],
                                                  func=AF.Silu))

            for c in range(2):
                gc = win * 2 + c
                cs = c * 128
                p.wait("scalar", eV)
                p.s(lambda e, cs=cs: e.activation(
                    out=DSC[:, 0:128], in_=ACU[:, cs:cs + 128], func=AF.Exp,
                    scale=-1.0, bias=ACU[:, cs + 127:cs + 128]))
                p.wait("vector", eS)
                p.v(lambda e, cs=cs: e.tensor_tensor(
                    out=DDS[:, :], in0=DTT[:, cs:cs + 128], in1=DSC[:, 0:128],
                    op=ALU.mult))
                p.wait("tensor", eV)
                p.wait("tensor", eS)
                for li, (dsto, neg) in enumerate(((0, False), (32, False),
                                                 (64, False), (96, True))):
                    bi, bk = B.take()
                    if dsto == 0:
                        p.t(lambda e, cs=cs, bk=bk: e.matmul(
                            bk[:, 0:32], ODW[:, cs:cs + 128], IDT[0:32, 0:32],
                            start=True, stop=True))
                    elif dsto == 32:
                        p.t(lambda e, cs=cs, bk=bk: e.matmul(
                            bk[:, 0:32], DTT[:, cs:cs + 128], IDT[0:32, 0:32],
                            start=True, stop=True))
                    elif dsto == 64:
                        p.t(lambda e, bk=bk: e.matmul(
                            bk[:, 0:32], DDS[:, 0:128], IDT[0:32, 0:32],
                            start=True, stop=True))
                    else:
                        p.t(lambda e, cs=cs, bk=bk: e.matmul(
                            bk[:, 0:32], ACU[:, cs:cs + 128], IDT[0:32, 0:32],
                            start=True, stop=True))
                    p.wait("vector", eT)
                    if neg:
                        p.v(lambda e, dsto=dsto, bk=bk: e.tensor_scalar(
                            out=TMJ[:, dsto:dsto + 32], in0=bk[:, 0:32],
                            scalar1=-1.0, scalar2=None, op0=ALU.mult))
                    else:
                        p.v(lambda e, dsto=dsto, bk=bk: e.tensor_copy(
                            out=TMJ[:, dsto:dsto + 32], in_=bk[:, 0:32]))
                    B.done(bi, "vector")
                p.wait("tensor", eS)
                for ec in range(16):
                    bi, bk = B.take()
                    p.t(lambda e, ec=ec, cs=cs, bk=bk: e.transpose(
                        bk[:, 0:128], WS[:, ec, cs:cs + 128], IDT[:]))
                    if ec % 2 == 0:
                        p.wait("scalar", eT)
                        p.s(lambda e, ec=ec, bk=bk: e.copy(
                            out=XTOK[:, ec * 128:(ec + 1) * 128], in_=bk[:, 0:128]))
                        B.done(bi, "scalar")
                    else:
                        p.wait("vector", eT)
                        p.v(lambda e, ec=ec, bk=bk: e.tensor_copy(
                            out=XTOK[:, ec * 128:(ec + 1) * 128], in_=bk[:, 0:128]))
                        B.done(bi, "vector")
                bi, bk = B.take()
                p.t(lambda e, cs=cs, bk=bk: e.transpose(
                    bk[:, 0:128], WS[:, 16, cs:cs + 128], IDT[:]))
                p.wait("vector", eT)
                p.v(lambda e, bk=bk: e.tensor_copy(out=BTOK[:, :], in_=bk[:, 0:128]))
                B.done(bi, "vector")
                bi, bk = B.take()
                p.t(lambda e, cs=cs, bk=bk: e.matmul(
                    bk[:, 0:128], WS[:, 16, cs:cs + 128], WS[:, 17, cs:cs + 128],
                    start=True, stop=True))
                p.wait("vector", eT)
                p.v(lambda e, bk=bk: e.tensor_tensor(
                    out=GT[:, :], in0=bk[:, 0:128], in1=MPAD[:, 384:512], op=ALU.mult))
                B.done(bi, "vector")
                x3 = XTOK.ap().rearrange("p (h q) -> p h q", q=64)
                dttb = bass.AP(TMJ.ap().tensor, 32, [[TMJ.ap().ap[0][0], 128], [1, 32], [0, 64]])
                ddsb = bass.AP(TMJ.ap().tensor, 64, [[TMJ.ap().ap[0][0], 128], [1, 32], [0, 64]])
                p.v(lambda e, x3=x3, dttb=dttb: e.tensor_tensor(
                    out=XDF.ap().rearrange("p (h q) -> p h q", q=64), in0=x3, in1=dttb,
                    op=ALU.mult))
                p.v(lambda e, x3=x3, ddsb=ddsb: e.tensor_tensor(
                    out=XDS.ap().rearrange("p (h q) -> p h q", q=64), in0=x3, in1=ddsb,
                    op=ALU.mult))
                p.wait("tensor", eV)
                p.wait("scalar", sST)
                for hg in range(4):
                    p.wait("sync", eT)
                    p.dma(SC[0:1, 0:1024], ACU[hg * 8:(hg + 1) * 8, cs:cs + 128],
                          sem=sWX)
                    p.wait("tensor", sWX)
                    bcb = []
                    for j in range(2):
                        bi, bk = B.take()
                        B.reserved.add(bi)
                        p.t(lambda e, j=j, bk=bk: e.matmul(
                            bk[:, 0:512], ONES[0:1, 0:128], SC[0:1, j * 512:(j + 1) * 512],
                            start=True, stop=False))
                        p.t(lambda e, bk=bk: e.matmul(
                            bk[:, 0:512], IDT[:, 0:128], MINF[:, 0:512],
                            start=False, stop=True))
                        bcb.append((bi, bk))
                    for hh in range(8):
                        h = hg * 8 + hh
                        bi, bk = bcb[hh // 4]
                        mt = MT0 if h % 2 == 0 else MT1
                        p.wait("scalar", eT)
                        p.wait("scalar", eV)
                        p.s(lambda e, h=h, hh=hh, bk=bk, mt=mt: e.activation(
                            out=mt[:, :], in_=bk[:, (hh % 4) * 128:(hh % 4) * 128 + 128],
                            func=AF.Exp, bias=TMJ[:, 96 + h:97 + h]))
                        p.wait("vector", eS)
                        p.v(lambda e, mt=mt: e.tensor_tensor(
                            out=mt[:, :], in0=mt[:, :], in1=GT[:, :], op=ALU.mult))
                        bi2, bk2 = B.take()
                        p.t(lambda e, h=h, cs=cs, bk2=bk2: e.matmul(
                            bk2[:, 0:64], WS[:, 17, cs:cs + 128], ST[:, h * 64:(h + 1) * 64],
                            start=True, stop=True))
                        p.wait("scalar", eT)
                        p.s(lambda e, h=h, bk2=bk2: e.activation(
                            out=YTL[:, h * 64:(h + 1) * 64], in_=bk2[:, 0:64],
                            func=AF.Copy, scale=TMJ[:, h:h + 1]))
                        B.done(bi2, "scalar")
                        bi3, bk3 = B.take()
                        p.wait("tensor", eV)
                        p.t(lambda e, h=h, mt=mt, bk3=bk3: e.matmul(
                            bk3[:, 0:64], mt[:, :], XDF[:, h * 64:(h + 1) * 64],
                            start=True, stop=True))
                        p.wait("vector", eT)
                        p.wait("vector", eS)
                        p.v(lambda e, h=h, bk3=bk3: e.tensor_tensor(
                            out=YTL[:, h * 64:(h + 1) * 64], in0=YTL[:, h * 64:(h + 1) * 64],
                            in1=bk3[:, 0:64], op=ALU.add))
                        B.done(bi3, "vector")
                    for bi, bk in bcb:
                        B.reserved.discard(bi)
                        B.done(bi, "scalar")
                p.wait("vector", eS)
                dpb = bass.AP(DPB.ap().tensor, 0, [[DPB.ap().ap[0][0], 128], [1, 32], [0, 64]])
                p.v(lambda e, x3=x3, dpb=dpb: e.tensor_tensor(
                    out=XDF.ap().rearrange("p (h q) -> p h q", q=64), in0=x3, in1=dpb,
                    op=ALU.mult))
                p.v(lambda e: e.tensor_tensor(out=YTL[:, :], in0=YTL[:, :], in1=XDF[:, :],
                                              op=ALU.add))
                p.wait("sync", eV)
                p.dma(dr["ydr"][gc * 128:(gc + 1) * 128, :], YTL[:, :], sem=sST)
                p.wait("tensor", eV)
                for j in range(4):
                    bkj = B.at(j)
                    B.reserved.add(j)
                    p.t(lambda e, j=j, bkj=bkj: e.matmul(
                        bkj[:, 0:512], BTOK[:, :], XDS[:, j * 512:(j + 1) * 512],
                        start=True, stop=True))
                bi5, bk5 = B.take()
                p.t(lambda e, cs=cs, bk5=bk5: e.matmul(
                    bk5[0:1, 0:32], ODW[:, cs + 127:cs + 128], IDT[0:32, 0:32],
                    start=True, stop=True))
                p.wait("vector", eT)
                p.v(lambda e, bk5=bk5: e.tensor_copy(out=GT[0:1, 0:32], in_=bk5[0:1, 0:32]))
                B.done(bi5, "vector")
                gb = bass.AP(GT.ap().tensor, 0, [[GT.ap().ap[0][0], 1], [1, 32], [0, 64]])
                p.v(lambda e, gb=gb: e.tensor_copy(
                    out=bass.AP(XTOK.ap().tensor, 0,
                                [[XTOK.ap().ap[0][0], 1], [64, 32], [1, 64]]), in_=gb))
                p.wait("tensor", eV)
                cdbanks = [B.at(4), B.at(5), PL[0], PL[1]]
                for j in range(4):
                    p.t(lambda e, j=j, b=cdbanks[j]: e.matmul(
                        b[:, 0:512], ONES[0:1, 0:128], XTOK[0:1, j * 512:(j + 1) * 512],
                        start=True, stop=True))
                p.wait("vector", eT)
                for j in range(4):
                    p.v(lambda e, j=j, b=cdbanks[j]: e.tensor_tensor(
                        out=ST[:, j * 512:(j + 1) * 512], in0=ST[:, j * 512:(j + 1) * 512],
                        in1=b[:, 0:512], op=ALU.mult))
                    if j < 2:
                        B.done(4 + j, "vector")
                for j in range(4):
                    p.v(lambda e, j=j, b=B.banks[j]: e.tensor_tensor(
                        out=ST[:, j * 512:(j + 1) * 512], in0=ST[:, j * 512:(j + 1) * 512],
                        in1=b[:, 0:512], op=ALU.add))
                    B.reserved.discard(j)
                    B.done(j, "vector")
                p.wait("tensor", eV)
        p.barrier()
        ls.close()

        ls = contextlib.ExitStack()
        ZT = p.sbuf_in(ls, "mb_ZT", (128, 2048))
        YTL2 = p.sbuf_in(ls, "mb_YTL2", (128, 2048))
        NWR = p.sbuf_in(ls, "mb_NWR", (128, 2048))
        MST = p.sbuf_in(ls, "mb_MST", (128, 8))
        W0B = p.sbuf_in(ls, "mb_W0B", (128, 8192))
        W1B = p.sbuf_in(ls, "mb_W1B", (128, 8192))
        for kk in range(DC):
            p.dma(W0B[:, kk * 1024:(kk + 1) * 1024],
                  dr["mb_inwT"][kk * 128:(kk + 1) * 128, 0:1024])
            p.dma(W1B[:, kk * 1024:(kk + 1) * 1024],
                  dr["mb_inwT"][kk * 128:(kk + 1) * 128, 1024:2048])
        p.dma(NWR[:], dr["mb_nwrep"][:])
        p.wait("tensor", sDMA)
        p.wait("vector", sDMA)
        sYL = p.sem("sYL")
        for gc in range(TT):
            p.wait("sync", eV)
            p.dma(YTL2[:, :], dr["ydr"][gc * 128:(gc + 1) * 128, :], sem=sYL)
            for zb in range(4):
                wb = W0B if zb < 2 else W1B
                zo = (zb % 2) * 512
                bi, bk = B.take()
                for kk in range(DC):
                    p.t(lambda e, kk=kk, gc=gc, zo=zo, wb=wb, bk=bk: e.matmul(
                        bk[:, 0:512], XLT[:, kk, gc * 128:(gc + 1) * 128],
                        wb[:, kk * 1024 + zo: kk * 1024 + zo + 512],
                        start=(kk == 0), stop=(kk == DC - 1)))
                p.wait("scalar", eT)
                p.s(lambda e, zb=zb, bk=bk: e.activation(
                    out=ZT[:, zb * 512:(zb + 1) * 512], in_=bk[:, 0:512], func=AF.Silu))
                B.done(bi, "scalar")
            p.wait("vector", eS)
            p.wait("vector", sYL)
            p.v(lambda e: e.tensor_tensor(out=YTL2[:, :], in0=YTL2[:, :], in1=ZT[:, :],
                                          op=ALU.mult))
            p.v(lambda e: e.tensor_tensor(out=ZT[:, :], in0=YTL2[:, :], in1=YTL2[:, :],
                                          op=ALU.mult))
            p.v(lambda e: e.tensor_reduce(out=MST[:, 0:1], in_=ZT[:, :], axis=AXX,
                                          op=ALU.add))
            p.v(lambda e: e.tensor_scalar(out=MST[:, 0:1], in0=MST[:, 0:1],
                                          scalar1=1.0 / 2048.0, scalar2=EPS,
                                          op0=ALU.mult, op1=ALU.add))
            rstd_newton(MST[:, 0:1], MST[:, 1:2], MST[:, 2:3], MST[:, 3:4])
            p.v(lambda e: e.tensor_scalar(out=YTL2[:, :], in0=YTL2[:, :],
                                          scalar1=MST[:, 1:2], scalar2=None, op0=ALU.mult))
            p.v(lambda e: e.tensor_tensor(out=YTL2[:, :], in0=YTL2[:, :], in1=NWR[:, :],
                                          op=ALU.mult))
            p.wait("sync", eV)
            p.dma(dr["ydr"][gc * 128:(gc + 1) * 128, :], YTL2[:, :], sem=sST)
            p.wait("sync", sST)
        p.barrier()
        ls.close()

        ls = contextlib.ExitStack()
        YTL3 = p.sbuf_in(ls, "mb_YTL3", (128, 2048))
        YT2 = p.sbuf_in(ls, "mb_YT2", (128, 16, 128))
        W0B = p.sbuf_in(ls, "mb2_W0B", (128, 8192))
        W1B = p.sbuf_in(ls, "mb2_W1B", (128, 8192))
        for ee in range(16):
            wb = W0B if ee < 8 else W1B
            p.dma(wb[:, (ee % 8) * 1024:(ee % 8 + 1) * 1024],
                  dr["mb_outwT"][ee * 128:(ee + 1) * 128, :])
        p.wait("tensor", sDMA)
        for gc in range(TT):
            p.wait("sync", eT)
            p.dma(YTL3[:, :], dr["ydr"][gc * 128:(gc + 1) * 128, :], sem=sYL)
            p.wait("sync", sST)
            p.dma(XT[:, 1, :], src[gc * 128:(gc + 1) * 128, :], sem=sXD)
            p.wait("tensor", sYL)
            for ee in range(16):
                bi, bk = B.take()
                p.t(lambda e, ee=ee, bk=bk: e.transpose(
                    bk[:, 0:128], YTL3[:, ee * 128:(ee + 1) * 128], IDT[:]))
                if ee % 2 == 0:
                    p.wait("scalar", eT)
                    p.s(lambda e, ee=ee, bk=bk: e.copy(out=YT2[:, ee, :], in_=bk[:, 0:128]))
                    B.done(bi, "scalar")
                else:
                    p.wait("vector", eT)
                    p.v(lambda e, ee=ee, bk=bk: e.tensor_copy(out=YT2[:, ee, :], in_=bk[:, 0:128]))
                    B.done(bi, "vector")
            p.wait("tensor", eS)
            p.wait("tensor", eV)
            for fb in range(2):
                bi, bk = B.take()
                for ee in range(16):
                    wb = W0B if ee < 8 else W1B
                    p.t(lambda e, ee=ee, fb=fb, wb=wb, bk=bk: e.matmul(
                        bk[:, 0:512], YT2[:, ee, :],
                        wb[:, (ee % 8) * 1024 + fb * 512: (ee % 8) * 1024 + fb * 512 + 512],
                        start=(ee == 0), stop=(ee == 15)))
                p.wait("vector", eT)
                p.wait("vector", sXD)
                p.v(lambda e, fb=fb, bk=bk: e.tensor_tensor(
                    out=XT[:, 1, fb * 512:(fb + 1) * 512],
                    in0=XT[:, 1, fb * 512:(fb + 1) * 512],
                    in1=bk[:, 0:512], op=ALU.add))
                B.done(bi, "vector")
            p.wait("sync", eV)
            p.dma(dst[gc * 128:(gc + 1) * 128, :], XT[:, 1, :], sem=sST)
        p.barrier()
        ls.close()


    def l1_attn_a(src, dst):
        ls = contextlib.ExitStack()
        QT = p.sbuf_in(ls, "la_QT", (128, NT))
        KT = p.sbuf_in(ls, "la_KT", (128, NT))
        VT = p.sbuf_in(ls, "la_VT", (128, 16, 130))
        ATT = p.sbuf_in(ls, "la_ATT", (128, NT))
        ATTB = p.sbuf_in(ls, "la_ATTB", (64, NT))
        TSA = p.sbuf_in(ls, "la_TSA", (128, 512))
        TSB = p.sbuf_in(ls, "la_TSB", (128, 512))
        WPR = p.sbuf_in(ls, "la_WPR", (128, 8, 384))
        GTS = p.sbuf_in(ls, "la_GTS", (16, NT))
        WGB = p.sbuf_in(ls, "la_WGB", (128, 8, 16))
        sc = SC

        for kk in range(DC):
            p.dma(WGB[:, kk, :], dr["la_wg"][kk * 128:(kk + 1) * 128, :])
        p.wait("tensor", sDMA)
        for tb in range(4):
            bi, bk = B.take()
            for kk in range(DC):
                p.t(lambda e, kk=kk, tb=tb, bk=bk: e.matmul(
                    bk[0:16, 0:512], WGB[:, kk, :], XLT[:, kk, tb * 512:(tb + 1) * 512],
                    start=(kk == 0), stop=(kk == DC - 1)))
            p.wait("scalar", eT)
            p.s(lambda e, tb=tb, bk=bk: e.activation(
                out=GTS[:, tb * 512:(tb + 1) * 512], in_=bk[0:16, 0:512], func=AF.Sigmoid))
            B.done(bi, "scalar")

        sAT = p.sem("sAT")
        for pp in range(8):
            h2 = 2 * pp
            p.wait("sync", eT)
            for kk in range(DC):
                p.dma(WPR[:, kk, 0:128],
                      dr["la_wqkv"][kk * 128:(kk + 1) * 128, h2 * 64:h2 * 64 + 128], sem=sAT)
                p.dma(WPR[:, kk, 128:256],
                      dr["la_wqkv"][kk * 128:(kk + 1) * 128, 1024 + h2 * 64:1024 + h2 * 64 + 128], sem=sAT)
                p.dma(WPR[:, kk, 256:384],
                      dr["la_wqkv"][kk * 128:(kk + 1) * 128, 2048 + h2 * 64:2048 + h2 * 64 + 128], sem=sAT)
            p.wait("tensor", sAT)
            for (dst_t, wo) in ((QT, 0), (KT, 128)):
                for tb in range(4):
                    bi, bk = B.take()
                    for kk in range(DC):
                        p.t(lambda e, kk=kk, tb=tb, wo=wo, bk=bk: e.matmul(
                            bk[:, 0:512], WPR[:, kk, wo:wo + 128],
                            XLT[:, kk, tb * 512:(tb + 1) * 512],
                            start=(kk == 0), stop=(kk == DC - 1)))
                    if tb % 2 == 0:
                        p.wait("scalar", eT)
                        p.s(lambda e, tb=tb, bk=bk, dst_t=dst_t: e.copy(
                            out=dst_t[:, tb * 512:(tb + 1) * 512], in_=bk[:, 0:512]))
                        B.done(bi, "scalar")
                    else:
                        p.wait("vector", eT)
                        p.v(lambda e, tb=tb, bk=bk, dst_t=dst_t: e.tensor_copy(
                            out=dst_t[:, tb * 512:(tb + 1) * 512], in_=bk[:, 0:512]))
                        B.done(bi, "vector")
            p.v(lambda e: e.memset(VT[:, :, 64:65], 1.0))
            p.v(lambda e: e.memset(VT[:, :, 129:130], 1.0))
            for ti in range(TT):
                bi, bk = B.take()
                for kk in range(DC):
                    p.t(lambda e, kk=kk, ti=ti, bk=bk: e.matmul(
                        bk[:, 0:128], XLT[:, kk, ti * 128:(ti + 1) * 128],
                        WPR[:, kk, 256:384], start=(kk == 0), stop=(kk == DC - 1)))
                if ti % 2 == 0:
                    p.wait("scalar", eT)
                    p.s(lambda e, ti=ti, bk=bk: e.copy(out=VT[:, ti, 0:64], in_=bk[:, 0:64]))
                    p.s(lambda e, ti=ti, bk=bk: e.copy(out=VT[:, ti, 65:129], in_=bk[:, 64:128]))
                    B.done(bi, "scalar")
                else:
                    p.wait("vector", eT)
                    p.v(lambda e, ti=ti, bk=bk: e.tensor_copy(out=VT[:, ti, 0:64], in_=bk[:, 0:64]))
                    p.v(lambda e, ti=ti, bk=bk: e.tensor_copy(out=VT[:, ti, 65:129], in_=bk[:, 64:128]))
                    B.done(bi, "vector")
            p.wait("tensor", eS)
            p.wait("tensor", eV)
            for qb in range(4):
                biA, pvA = B.take()
                B.reserved.add(biA)
                biB, pvB = B.take()
                B.reserved.add(biB)
                nkt = 4 * (qb + 1)
                for kt in range(nkt):
                    bi, bk = B.take()
                    p.t(lambda e, kt=kt, qb=qb, bk=bk: e.matmul(
                        bk[:, 0:512], KT[0:64, kt * 128:(kt + 1) * 128],
                        QT[0:64, qb * 512:(qb + 1) * 512], start=True, stop=True))
                    bi2, bk2 = B.take()
                    p.t(lambda e, kt=kt, qb=qb, bk2=bk2: e.matmul(
                        bk2[:, 0:512], KT[64:128, kt * 128:(kt + 1) * 128],
                        QT[64:128, qb * 512:(qb + 1) * 512], start=True, stop=True,
                        tile_position=(64, 0)))
                    p.wait("scalar", eT)
                    p.s(lambda e, bk=bk: e.activation(
                        out=TSA[:, :], in_=bk[:, 0:512], func=AF.Tanh, scale=0.125 / 50.0))
                    p.s(lambda e, bk2=bk2: e.activation(
                        out=TSB[:, :], in_=bk2[:, 0:512], func=AF.Tanh, scale=0.125 / 50.0))
                    p.s(lambda e: e.activation(
                        out=TSA[:, :], in_=TSA[:, :], func=AF.Exp, scale=50.0, bias=-50.0))
                    p.s(lambda e: e.activation(
                        out=TSB[:, :], in_=TSB[:, :], func=AF.Exp, scale=50.0, bias=-50.0))
                    B.done(bi, "scalar")
                    B.done(bi2, "scalar")
                    g = kt * 128 - qb * 512
                    if 0 <= g < 512:
                        mslice = MPAD[:, 384 - g:896 - g]
                        p.wait("vector", eS)
                        p.v(lambda e, mslice=mslice: e.tensor_tensor(
                            out=TSA[:, :], in0=TSA[:, :], in1=mslice, op=ALU.mult))
                        p.v(lambda e, mslice=mslice: e.tensor_tensor(
                            out=TSB[:, :], in0=TSB[:, :], in1=mslice, op=ALU.mult))
                        p.wait("tensor", eV)
                    else:
                        p.wait("tensor", eS)
                    p.t(lambda e, kt=kt, pvA=pvA, nkt=nkt: e.matmul(
                        pvA[0:65, 0:512], VT[:, kt, 0:65], TSA[:, :],
                        start=(kt == 0), stop=(kt == nkt - 1)))
                    p.t(lambda e, kt=kt, pvB=pvB, nkt=nkt: e.matmul(
                        pvB[0:65, 0:512], VT[:, kt, 65:130], TSB[:, :],
                        start=(kt == 0), stop=(kt == nkt - 1)))
                p.wait("sync", eV)
                p.wait("sync", eT)
                p.dma(sc[0:1, 0:512], GTS[h2:h2 + 1, qb * 512:(qb + 1) * 512], sem=sAT)
                p.dma(sc[0:1, 512:1024], GTS[h2 + 1:h2 + 2, qb * 512:(qb + 1) * 512], sem=sAT)
                p.wait("vector", sAT)
                p.wait("vector", eT)
                p.v(lambda e, pvA=pvA: e.reciprocal(out=XT[0:1, 0, 0:512], in_=pvA[64:65, 0:512]))
                p.v(lambda e, pvB=pvB: e.reciprocal(out=XT[0:1, 0, 512:1024], in_=pvB[64:65, 0:512]))
                p.v(lambda e: e.tensor_tensor(out=sc[0:1, 0:512], in0=sc[0:1, 0:512],
                                              in1=XT[0:1, 0, 0:512], op=ALU.mult))
                p.v(lambda e: e.tensor_tensor(out=sc[0:1, 512:1024], in0=sc[0:1, 512:1024],
                                              in1=XT[0:1, 0, 512:1024], op=ALU.mult))
                bi3, bk3 = B.take()
                bi4, bk4 = B.take()
                p.wait("tensor", eV)
                p.t(lambda e, bk3=bk3: e.matmul(
                    bk3[0:64, 0:512], ONES[0:1, 0:64], sc[0:1, 0:512],
                    start=True, stop=True))
                p.t(lambda e, bk4=bk4: e.matmul(
                    bk4[0:64, 0:512], ONES[0:1, 0:64], sc[0:1, 512:1024],
                    start=True, stop=True))
                p.wait("vector", eT)
                p.v(lambda e, bk3=bk3: e.tensor_copy(out=TSA[0:64, :], in_=bk3[0:64, 0:512]))
                p.v(lambda e, bk4=bk4: e.tensor_copy(out=TSB[0:64, :], in_=bk4[0:64, 0:512]))
                B.done(bi3, "vector")
                B.done(bi4, "vector")
                p.v(lambda e, qb=qb, pvA=pvA: e.tensor_tensor(
                    out=ATT[0:64, qb * 512:(qb + 1) * 512], in0=pvA[0:64, 0:512],
                    in1=TSA[0:64, :], op=ALU.mult))
                p.v(lambda e, qb=qb, pvB=pvB: e.tensor_tensor(
                    out=ATTB[0:64, qb * 512:(qb + 1) * 512], in0=pvB[0:64, 0:512],
                    in1=TSB[0:64, :], op=ALU.mult))
                B.reserved.discard(biA)
                B.reserved.discard(biB)
                B.done(biA, "vector")
                B.done(biB, "vector")
            p.wait("sync", eV)
            p.dma(dr["attdr"][pp * 128:pp * 128 + 64, :], ATT[0:64, :], sem=sST)
            p.dma(dr["attdr"][pp * 128 + 64:(pp + 1) * 128, :], ATTB[0:64, :], sem=sST)
            p.wait("vector", sST)
        p.barrier()
        ls.close()

        ls = contextlib.ExitStack()
        WOB = p.sbuf_in(ls, "la_WOB", (128, 8192))
        sAL = p.sem("sAL")
        for ppc in range(8):
            p.dma(XLT[:, ppc, :], dr["attdr"][ppc * 128:(ppc + 1) * 128, :], sem=sAL)
        for kk in range(DC):
            p.dma(WOB[:, kk * D:(kk + 1) * D], dr["la_wo"][kk * 128:(kk + 1) * 128, :])
        p.wait("tensor", sAL)
        p.wait("tensor", sDMA)
        for i in range(TT):
            p.wait("sync", sST)
            p.dma(XT[:, 1, :], src[i * 128:(i + 1) * 128, :], sem=sXD)
            p.wait("vector", sXD)
            for fb in range(2):
                bi, bk = B.take()
                for kk in range(DC):
                    p.t(lambda e, kk=kk, i=i, fb=fb, bk=bk: e.matmul(
                        bk[:, 0:512], XLT[:, kk, i * 128:(i + 1) * 128],
                        WOB[:, kk * D + fb * 512: kk * D + fb * 512 + 512],
                        start=(kk == 0), stop=(kk == DC - 1)))
                p.wait("vector", eT)
                p.v(lambda e, fb=fb, bk=bk: e.tensor_tensor(
                    out=XT[:, 1, fb * 512:(fb + 1) * 512],
                    in0=XT[:, 1, fb * 512:(fb + 1) * 512],
                    in1=bk[:, 0:512], op=ALU.add))
                B.done(bi, "vector")
            p.wait("sync", eV)
            p.dma(dst[i * 128:(i + 1) * 128, :], XT[:, 1, :], sem=sST)
        p.barrier()
        ls.close()


    QW = p.sbuf("QW", (128, 4))

    def quant_weight_cache(srcT, dstq, nchunk, width, c0, c1):
        ls = contextlib.ExitStack()
        WQS = p.sbuf_in(ls, "qw_WQS", (128, width))
        WQI = p.sbuf_in(ls, "qw_WQI", (128, width), I32)
        WQB = p.sbuf_in(ls, "qw_WQB", (128, width), BF16)
        QC = p.sbuf_in(ls, "qw_QC", (128, 40))
        sQW = p.sem("sQW")
        for ck in range(nchunk):
            p.wait("sync", eV)
            p.dma(WQS[:], srcT[ck * 128:(ck + 1) * 128, :], sem=sQW)
            p.wait("vector", sQW)
            p.v(lambda e: e.scalar_tensor_tensor(out=WQS[:, :], in0=WQS[:, :],
                                                 scalar=-1.0, in1=WQS[:, :],
                                                 op0=ALU.mult, op1=ALU.max))
            p.v(lambda e, ck=ck: e.tensor_reduce(out=QC[:, ck:ck + 1], in_=WQS[:, :],
                                                 axis=AXX, op=ALU.add))

        p.v(lambda e, nchunk=nchunk: e.tensor_reduce(
            out=QC[:, 36:37], in_=QC[:, 0:nchunk], axis=AXX, op=ALU.add))
        p.wait("tensor", eV)
        bi, bk = B.take()
        p.t(lambda e, bk=bk: e.matmul(bk[0:1, 0:1], QC[:, 36:37], ONES[:, 0:1],
                                      start=True, stop=True))
        p.wait("vector", eT)
        p.v(lambda e, bk=bk, nchunk=nchunk, width=width: e.tensor_scalar(
            out=QC[0:1, 37:38], in0=bk[0:1, 0:1],
            scalar1=1.0 / float(nchunk * 128 * width), scalar2=1e-5,
            op0=ALU.mult, op1=ALU.max))
        B.done(bi, "vector")
        bi, bk = B.take()
        p.wait("tensor", eV)
        p.t(lambda e, bk=bk: e.matmul(bk[0:128, 0:1], ONES[0:1, 0:128], QC[0:1, 37:38],
                                      start=True, stop=True))
        p.wait("vector", eT)
        p.v(lambda e, bk=bk, c1=c1: e.tensor_copy(out=QW[:, c1:c1 + 1], in_=bk[0:128, 0:1]))
        B.done(bi, "vector")
        p.v(lambda e, c0=c0, c1=c1: e.reciprocal(out=QW[:, c0:c0 + 1], in_=QW[:, c1:c1 + 1]))
        for ck in range(nchunk):
            p.wait("sync", eV)
            p.dma(WQS[:], srcT[ck * 128:(ck + 1) * 128, :], sem=sQW)
            p.wait("vector", sQW)
            p.v(lambda e, c0=c0: e.tensor_scalar(out=WQS[:, :], in0=WQS[:, :],
                                                 scalar1=QW[:, c0:c0 + 1], scalar2=None,
                                                 op0=ALU.mult))
            p.v(lambda e: e.tensor_copy(out=WQI[:, :], in_=WQS[:, :]))
            p.v(lambda e: e.tensor_copy(out=WQS[:, :], in_=WQI[:, :]))
            p.v(lambda e: e.tensor_scalar(out=WQS[:, :], in0=WQS[:, :], scalar1=1.0,
                                          scalar2=-1.0, op0=ALU.min, op1=ALU.max))
            p.v(lambda e: e.tensor_copy(out=WQB[:, :], in_=WQS[:, :]))
            p.wait("sync", eV)
            p.dma(dstq[ck * 128:(ck + 1) * 128, :], WQB[:, :], sem=sST)
        p.barrier()
        ls.close()

    def l1_ffn(src, dst, first=False):
        if first:
            quant_weight_cache(dr["lf_w1T"], dr["w1qdr"], DC, 4096, 0, 1)
            quant_weight_cache(dr["lf_w2T"], dr["w2qdr"], 32, 1024, 2, 3)
        ls = contextlib.ExitStack()
        XQTi = p.sbuf_in(ls, "lf_XQTi", (128, DC, NT), BF16)
        HCH = p.sbuf_in(ls, "lf_HCH", (128, NT))
        CVt = p.sbuf_in(ls, "lf_CV", (128, NT))
        AMX = p.sbuf_in(ls, "lf_AMX", (128, NT))
        AMN = p.sbuf_in(ls, "lf_AMN", (128, NT))
        SCB = p.sbuf_in(ls, "lf_SCB", (128, NT))
        RECB = p.sbuf_in(ls, "lf_RECB", (128, NT))
        XQ2i = p.sbuf_in(ls, "lf_XQ2i", (128, NT), BF16)
        W1QB = p.sbuf_in(ls, "lf_W1QB", (128, 2, 8, 128), BF16)
        W2QB = p.sbuf_in(ls, "lf_W2QB", (128, 2, 1024), BF16)
        XQB = p.sbuf_in(ls, "lf_XQB", (128, 2, 512), BF16)
        CWc = p.sbuf_in(ls, "lf_CWc", (128, 96))
        CBc = p.sbuf_in(ls, "lf_CBc", (128, 32))
        LFC = p.sbuf_in(ls, "lf_LFC", (128, 8))
        RXC = p.sbuf_in(ls, "lf_RXC", (128, 16))
        RX2 = p.sbuf_in(ls, "lf_RX2", (128, 16))
        ITI = p.sbuf_in(ls, "lf_ITI", (128, NT), I32)
        p.dma(CWc[:], dr["lf_cw"][:])
        p.dma(CBc[:], dr["lf_cb"][:])
        sLF = p.sem("sLF")

        for i in range(TT):
            t = xflat(i * D, [[1, D]])
            p.v(lambda e, t=t: e.reduce_max(out=LFC[:, 0:1], in_=t, axis=AXX,
                                            apply_absolute_value=True))
            p.v(lambda e: e.tensor_scalar(out=LFC[:, 1:2], in0=LFC[:, 0:1],
                                          scalar1=1e-5, scalar2=None, op0=ALU.max))
            p.v(lambda e: e.reciprocal(out=LFC[:, 2:3], in_=LFC[:, 1:2]))
            p.v(lambda e: e.tensor_scalar(out=LFC[:, 2:3], in0=LFC[:, 2:3],
                                          scalar1=127.0, scalar2=None, op0=ALU.mult))
            p.v(lambda e, i=i: e.tensor_scalar(out=RXC[:, i:i + 1], in0=LFC[:, 1:2],
                                               scalar1=1.0 / 127.0, scalar2=None,
                                               op0=ALU.mult))
            p.v(lambda e, t=t: e.tensor_scalar(out=t, in0=t, scalar1=LFC[:, 2:3],
                                               scalar2=None, op0=ALU.mult))
            p.v(lambda e, t=t: e.tensor_copy(out=ITI[:, 0:D], in_=t))
            p.v(lambda e, t=t: e.tensor_copy(out=t, in_=ITI[:, 0:D]))
            p.wait("tensor", eV)
            for j in range(DC):
                bi, bk = B.take()
                p.t(lambda e, i=i, j=j, bk=bk: e.transpose(
                    bk[:, 0:128], xflat(i * D + j * 128, [[1, 128]]), IDT[:]))
                p.wait("vector", eT)
                p.v(lambda e, i=i, j=j, bk=bk: e.tensor_copy(
                    out=XQTi[:, j, i * 128:(i + 1) * 128], in_=bk[:, 0:128]))
                B.done(bi, "vector")
        p.wait("tensor", eV)
        for i in range(TT):
            bi, bk = B.take()
            p.t(lambda e, i=i, bk=bk: e.matmul(bk[0:1, 0:128], RXC[:, i:i + 1],
                                               IDT[:], start=True, stop=True))
            p.wait("vector", eT)
            p.v(lambda e, i=i, bk=bk: e.tensor_copy(
                out=CVt[0:1, i * 128:(i + 1) * 128], in_=bk[0:1, 0:128]))
            B.done(bi, "vector")
        p.wait("tensor", eV)
        for j in range(4):
            bi, bk = B.take()
            p.t(lambda e, j=j, bk=bk: e.matmul(
                bk[:, 0:512], ONES[0:1, 0:128], CVt[0:1, j * 512:(j + 1) * 512],
                start=True, stop=True))
            p.wait("vector", eT)
            p.v(lambda e, j=j, bk=bk: e.tensor_copy(
                out=RECB[:, j * 512:(j + 1) * 512], in_=bk[:, 0:512]))
            B.done(bi, "vector")

        for ec in range(32):
            slot = ec % 2
            p.wait("sync", eT)
            for kk in range(DC):
                p.dma(W1QB[:, slot, kk, :],
                      dr["w1qdr"][kk * 128:(kk + 1) * 128, ec * 128:(ec + 1) * 128],
                      sem=sLF)
            p.wait("tensor", sLF)
            for tb in range(4):
                bi, bk = B.take()
                for kk in range(DC):
                    p.t(lambda e, kk=kk, tb=tb, slot=slot, bk=bk: e.matmul(
                        bk[:, 0:512], W1QB[:, slot, kk, :],
                        XQTi[:, kk, tb * 512:(tb + 1) * 512],
                        start=(kk == 0), stop=(kk == DC - 1)))
                p.wait("scalar", eT)
                p.s(lambda e, tb=tb, bk=bk: e.activation(
                    out=HCH[:, tb * 512:(tb + 1) * 512], in_=bk[:, 0:512],
                    func=AF.Copy, scale=QW[:, 1:2]))
                B.done(bi, "scalar")
            p.wait("vector", eS)
            p.v(lambda e: e.tensor_tensor(out=HCH[:, :], in0=HCH[:, :], in1=RECB[:, :],
                                          op=ALU.mult))
            p.v(lambda e, ec=ec: e.tensor_scalar(
                out=CVt[:, :], in0=HCH[:, :], scalar1=CWc[:, 32 + ec:33 + ec],
                scalar2=CBc[:, ec:ec + 1], op0=ALU.mult, op1=ALU.add))
            p.v(lambda e, ec=ec: e.scalar_tensor_tensor(
                out=CVt[:, 1:NT], in0=HCH[:, 0:NT - 1], scalar=CWc[:, ec:ec + 1],
                in1=CVt[:, 1:NT], op0=ALU.mult, op1=ALU.add))
            p.v(lambda e, ec=ec: e.scalar_tensor_tensor(
                out=CVt[:, 0:NT - 1], in0=HCH[:, 1:NT], scalar=CWc[:, 64 + ec:65 + ec],
                in1=CVt[:, 0:NT - 1], op0=ALU.mult, op1=ALU.add))
            p.wait("scalar", eV)
            p.s(lambda e: e.activation(out=HCH[:, :], in_=CVt[:, :], func=AF.Silu))
            p.wait("vector", eS)
            if ec == 0:
                p.v(lambda e: e.tensor_copy(out=AMX[:, :], in_=HCH[:, :]))
                p.v(lambda e: e.tensor_scalar(out=AMN[:, :], in0=HCH[:, :],
                                              scalar1=-1.0, scalar2=None, op0=ALU.mult))
            else:
                p.v(lambda e: e.tensor_tensor(out=AMX[:, :], in0=AMX[:, :], in1=HCH[:, :],
                                              op=ALU.max))
                p.v(lambda e: e.scalar_tensor_tensor(out=AMN[:, :], in0=HCH[:, :],
                                                     scalar=-1.0, in1=AMN[:, :],
                                                     op0=ALU.mult, op1=ALU.max))
            p.wait("sync", eV)
            p.wait("sync", sST)
            p.dma(dr["h2dr"][ec * 128:(ec + 1) * 128, :], HCH[:, :], sem=sST)

        p.v(lambda e: e.tensor_tensor(out=AMX[:, :], in0=AMX[:, :], in1=AMN[:, :],
                                      op=ALU.max))
        for i in range(TT):
            bi, bk = B.take()
            p.t(lambda e, i=i, bk=bk: e.transpose(
                bk[:, 0:128], AMX[:, i * 128:(i + 1) * 128], IDT[:]))
            p.wait("vector", eT)
            p.v(lambda e, i=i, bk=bk: e.tensor_reduce(
                out=RX2[:, i:i + 1], in_=bk[:, 0:128], axis=AXX, op=ALU.max))
            B.done(bi, "vector")
        p.v(lambda e: e.tensor_scalar(out=RX2[:, :], in0=RX2[:, :], scalar1=1e-5,
                                      scalar2=None, op0=ALU.max))
        p.v(lambda e: e.reciprocal(out=RXC[:, 0:16], in_=RX2[:, :]))
        p.v(lambda e: e.tensor_scalar(out=RXC[:, 0:16], in0=RXC[:, 0:16],
                                      scalar1=127.0, scalar2=None, op0=ALU.mult))
        p.v(lambda e: e.tensor_scalar(out=RX2[:, :], in0=RX2[:, :],
                                      scalar1=1.0 / 127.0, scalar2=None, op0=ALU.mult))
        p.wait("tensor", eV)
        for i in range(TT):
            bi, bk = B.take()
            p.t(lambda e, i=i, bk=bk: e.matmul(bk[0:1, 0:128], RXC[:, i:i + 1],
                                               IDT[:], start=True, stop=True))
            p.wait("vector", eT)
            p.v(lambda e, i=i, bk=bk: e.tensor_copy(
                out=CVt[0:1, i * 128:(i + 1) * 128], in_=bk[0:1, 0:128]))
            B.done(bi, "vector")
        p.wait("tensor", eV)
        for j in range(4):
            bi, bk = B.take()
            p.t(lambda e, j=j, bk=bk: e.matmul(
                bk[:, 0:512], ONES[0:1, 0:128], CVt[0:1, j * 512:(j + 1) * 512],
                start=True, stop=True))
            p.wait("vector", eT)
            p.v(lambda e, j=j, bk=bk: e.tensor_copy(
                out=SCB[:, j * 512:(j + 1) * 512], in_=bk[:, 0:512]))
            B.done(bi, "vector")

        ones_b = bass.AP(ONES.ap().tensor, 0, [[ONES.ap().ap[0][0], 128], [0, NT]])
        for ec in range(32):
            p.wait("sync", eV)
            p.dma(HCH[:, :], dr["h2dr"][ec * 128:(ec + 1) * 128, :], sem=sLF)
            p.wait("vector", sLF)
            p.v(lambda e: e.tensor_tensor(out=CVt[:, :], in0=HCH[:, :], in1=SCB[:, :],
                                          op=ALU.mult))
            p.v(lambda e: e.tensor_copy(out=ITI[:, :], in_=CVt[:, :]))
            p.v(lambda e: e.tensor_copy(out=XQ2i[:, :], in_=ITI[:, :]))
            p.wait("sync", eV)
            p.wait("sync", sST)
            p.dma(dr["xq2dr"][ec * 128:(ec + 1) * 128, :], XQ2i[:, :], sem=sST)

        p.barrier()
        yps = [B.at(j) for j in range(6)] + [PL[0], PL[1]]
        for j in range(6):
            B.reserved.add(j)
        for tb in range(4):
            p.wait("tensor", eS)
            for ec in range(32):
                slot = ec % 2
                p.wait("sync", eT)
                p.dma(XQB[:, slot, :],
                      dr["xq2dr"][ec * 128:(ec + 1) * 128, tb * 512:(tb + 1) * 512],
                      sem=sLF)
                p.dma(W2QB[:, slot, :], dr["w2qdr"][ec * 128:(ec + 1) * 128, :],
                      sem=sLF)
                p.wait("tensor", sLF)
                for ti in range(4):
                    for fb in range(2):
                        pj = ti * 2 + fb
                        p.t(lambda e, ec=ec, ti=ti, fb=fb, slot=slot, pj=pj: e.matmul(
                            yps[pj][:, 0:512], XQB[:, slot, ti * 128:(ti + 1) * 128],
                            W2QB[:, slot, fb * 512:(fb + 1) * 512],
                            start=(ec == 0), stop=(ec == 31)))
            for ti in range(4):
                gi = tb * 4 + ti
                p.wait("sync", sST)
                p.dma(XT[:, 1, :], src[gi * 128:(gi + 1) * 128, :], sem=sXD)
                p.wait("vector", eS)
                p.v(lambda e, gi=gi: e.tensor_tensor(
                    out=LFC[:, 3:4], in0=RX2[:, gi:gi + 1], in1=QW[:, 3:4],
                    op=ALU.mult))
                p.wait("scalar", eV)
                p.wait("scalar", eT)
                for fb in range(2):
                    pj = ti * 2 + fb
                    p.s(lambda e, fb=fb, pj=pj: e.activation(
                        out=CVt[:, fb * 512:(fb + 1) * 512], in_=yps[pj][:, 0:512],
                        func=AF.Copy, scale=LFC[:, 3:4]))
                p.wait("vector", sXD)
                p.wait("vector", eS)
                p.v(lambda e: e.tensor_tensor(out=XT[:, 1, 0:1024], in0=XT[:, 1, 0:1024],
                                              in1=CVt[:, 0:1024], op=ALU.add))
                p.wait("sync", eV)
                p.dma(dst[gi * 128:(gi + 1) * 128, :], XT[:, 1, :], sem=sST)
        for j in range(6):
            B.reserved.discard(j)
            B.done(j, "scalar")
        p.barrier()
        ls.close()


    def l1_attn_b(src, dst):
        ls = contextlib.ExitStack()
        QTb = p.sbuf_in(ls, "lb_QT", (128, NT))
        KTb = p.sbuf_in(ls, "lb_KT", (128, NT))
        VTOK = p.sbuf_in(ls, "lb_VT", (128, 16, 128))
        ZTOK = p.sbuf_in(ls, "lb_ZT", (128, 16, 128))
        Z2T = p.sbuf_in(ls, "lb_Z2", (128, 16, 128))
        Q0T = p.sbuf_in(ls, "lb_Q0", (128, 16, 128))
        DZ1 = p.sbuf_in(ls, "lb_DZ1", (128, 16, 128))
        SH = p.sbuf_in(ls, "lb_SH", (128, 16, 128))
        SW = p.sbuf_in(ls, "lb_SW", (128, 16, 128))
        MHA = p.sbuf_in(ls, "lb_MHA", (64, 512))
        MHB = p.sbuf_in(ls, "lb_MHB", (64, 512))
        KKS = p.sbuf_in(ls, "lb_KKS", (128, 2, 512))
        WPR = p.sbuf_in(ls, "lb_WPR", (128, 8, 384))
        W0D = p.sbuf_in(ls, "lb_W0D", (128, 64))
        BPR = p.sbuf_in(ls, "lb_BPR", (1, 3072))
        LNGR = p.sbuf_in(ls, "lb_LNGR", (128, 128))
        LNBR = p.sbuf_in(ls, "lb_LNBR", (128, 128))
        GNR = p.sbuf_in(ls, "lb_GNR", (128, 1024, 2))
        STT = p.sbuf_in(ls, "lb_STT", (128, 64))

        p.dma(W0D[:], dr["lb_w0d"][:])
        p.dma(BPR[:], dr["lb_bproj"][:])
        p.dma(LNGR[:], dr["lb_lngrep"][:])
        p.dma(LNBR[:], dr["lb_lnbrep"][:])
        p.dma(GNR[:, :, :], dr["lb_gngbrep"][:])
        sLB = p.sem("sLB")

        z4 = ZTOK.ap().rearrange("p a (h q) -> p a h q", q=64)
        z24 = Z2T.ap().rearrange("p a (h q) -> p a h q", q=64)
        v4 = VTOK.ap().rearrange("p a (h q) -> p a h q", q=64)
        sh4 = SH.ap().rearrange("p a (h q) -> p a h q", q=64)
        sw4 = SW.ap().rearrange("p a (h q) -> p a h q", q=64)
        dz4 = DZ1.ap().rearrange("p a (h q) -> p a h q", q=64)
        pst = STT.ap().ap[0][0]

        def st3(c0):
            return bass.AP(STT.ap().tensor, c0, [[pst, 128], [2, 16], [1, 2], [0, 64]])

        lngb = bass.AP(LNGR.ap().tensor, 0, [[LNGR.ap().ap[0][0], 128], [0, 16], [1, 128]])
        lnbb = bass.AP(LNBR.ap().tensor, 0, [[LNBR.ap().ap[0][0], 128], [0, 16], [1, 128]])

        def ln_vjp(ZT, zt4, DZO, dzo4):
            p.v(lambda e: e.tensor_reduce(out=STT[:, 0:32], in_=zt4, axis=AXX, op=ALU.add))
            p.v(lambda e: e.tensor_scalar(out=STT[:, 0:32], in0=STT[:, 0:32],
                                          scalar1=1.0 / 64.0, scalar2=None, op0=ALU.mult))
            p.v(lambda e: e.tensor_tensor(out=sh4, in0=zt4, in1=st3(0), op=ALU.subtract))
            p.v(lambda e: e.tensor_tensor(out=sw4, in0=sh4, in1=sh4, op=ALU.mult))
            p.v(lambda e: e.tensor_reduce(out=STT[:, 32:64], in_=sw4, axis=AXX, op=ALU.add))
            p.v(lambda e: e.tensor_scalar(out=STT[:, 32:64], in0=STT[:, 32:64],
                                          scalar1=1.0 / 64.0, scalar2=EPS,
                                          op0=ALU.mult, op1=ALU.add))
            p.wait("scalar", eV)
            p.s(lambda e: e.activation(out=SC[:, 0:32], in_=STT[:, 32:64], func=AF.Ln))
            p.s(lambda e: e.activation(out=SC[:, 32:64], in_=SC[:, 0:32], func=AF.Exp,
                                       scale=-0.5))
            p.wait("vector", eS)
            p.v(lambda e: e.tensor_tensor(out=SC[:, 0:32], in0=SC[:, 32:64], in1=SC[:, 32:64],
                                          op=ALU.mult))
            p.v(lambda e: e.tensor_tensor(out=SC[:, 0:32], in0=SC[:, 0:32], in1=STT[:, 32:64],
                                          op=ALU.mult))
            p.v(lambda e: e.tensor_scalar(out=SC[:, 0:32], in0=SC[:, 0:32],
                                          scalar1=-0.5, scalar2=1.5, op0=ALU.mult, op1=ALU.add))
            p.v(lambda e: e.tensor_tensor(out=STT[:, 32:64], in0=SC[:, 32:64], in1=SC[:, 0:32],
                                          op=ALU.mult))
            p.v(lambda e: e.tensor_tensor(out=sh4, in0=sh4, in1=st3(32), op=ALU.mult))
            p.v(lambda e: e.tensor_tensor(out=sw4, in0=sh4, in1=lngb, op=ALU.mult))
            p.v(lambda e: e.tensor_tensor(out=sw4, in0=sw4, in1=lnbb, op=ALU.add))
            p.v(lambda e: e.tensor_tensor(out=sw4, in0=sw4, in1=v4, op=ALU.subtract))
            p.v(lambda e: e.tensor_tensor(out=sw4, in0=sw4, in1=lngb, op=ALU.mult))
            p.v(lambda e: e.tensor_reduce(out=STT[:, 0:32], in_=sw4, axis=AXX, op=ALU.add))
            p.v(lambda e: e.tensor_scalar(out=STT[:, 0:32], in0=STT[:, 0:32],
                                          scalar1=1.0 / 64.0, scalar2=None, op0=ALU.mult))
            p.v(lambda e: e.tensor_tensor(out=dzo4, in0=sw4, in1=sh4, op=ALU.mult))
            p.v(lambda e: e.tensor_reduce(out=SC[:, 0:32], in_=dzo4, axis=AXX, op=ALU.add))
            p.v(lambda e: e.tensor_scalar(out=SC[:, 0:32], in0=SC[:, 0:32],
                                          scalar1=1.0 / 64.0, scalar2=None, op0=ALU.mult))
            sc3 = bass.AP(SC.ap().tensor, 0, [[SC.ap().ap[0][0], 128], [2, 16], [1, 2], [0, 64]])
            p.v(lambda e, sc3=sc3: e.tensor_tensor(out=sh4, in0=sh4, in1=sc3, op=ALU.mult))
            p.v(lambda e: e.tensor_tensor(out=sw4, in0=sw4, in1=st3(0), op=ALU.subtract))
            p.v(lambda e: e.tensor_tensor(out=sw4, in0=sw4, in1=sh4, op=ALU.subtract))
            p.v(lambda e: e.tensor_tensor(out=dzo4, in0=sw4, in1=st3(32), op=ALU.mult))

        def strips(SRC_T, DZS, subtract_into):
            p.wait("tensor", eV)
            for tb in range(4):
                biA, psA = B.take()
                B.reserved.add(biA)
                biB, psB = B.take()
                B.reserved.add(biB)
                nsc = 4 * (tb + 1)
                for sc_ in range(nsc):
                    bi, bk = B.take()
                    p.t(lambda e, sc_=sc_, tb=tb, bk=bk: e.matmul(
                        bk[:, 0:512], KTb[0:64, sc_ * 128:(sc_ + 1) * 128],
                        SRC_T[0:64, tb * 512:(tb + 1) * 512], start=True, stop=True))
                    bi2, bk2 = B.take()
                    p.t(lambda e, sc_=sc_, tb=tb, bk2=bk2: e.matmul(
                        bk2[:, 0:512], KTb[64:128, sc_ * 128:(sc_ + 1) * 128],
                        SRC_T[64:128, tb * 512:(tb + 1) * 512], start=True, stop=True,
                        tile_position=(64, 0)))
                    g = sc_ * 128 - tb * 512
                    p.wait("vector", eT)
                    if 0 <= g < 512:
                        msl = MPAD[:, 384 - g:896 - g]
                        p.v(lambda e, bk=bk, msl=msl: e.tensor_tensor(
                            out=KKS[:, 0, :], in0=bk[:, 0:512], in1=msl, op=ALU.mult))
                        p.v(lambda e, bk2=bk2, msl=msl: e.tensor_tensor(
                            out=KKS[:, 1, :], in0=bk2[:, 0:512], in1=msl, op=ALU.mult))
                    else:
                        p.v(lambda e, bk=bk: e.tensor_copy(out=KKS[:, 0, :], in_=bk[:, 0:512]))
                        p.v(lambda e, bk2=bk2: e.tensor_copy(out=KKS[:, 1, :], in_=bk2[:, 0:512]))
                    B.done(bi, "vector")
                    B.done(bi2, "vector")
                    p.wait("tensor", eV)
                    p.t(lambda e, sc_=sc_, psA=psA, nsc=nsc: e.matmul(
                        psA[0:64, 0:512], DZS[:, sc_, 0:64], KKS[:, 0, :],
                        start=(sc_ == 0), stop=(sc_ == nsc - 1)))
                    p.t(lambda e, sc_=sc_, psB=psB, nsc=nsc: e.matmul(
                        psB[0:64, 0:512], DZS[:, sc_, 64:128], KKS[:, 1, :],
                        start=(sc_ == 0), stop=(sc_ == nsc - 1)))
                p.wait("scalar", eT)
                p.s(lambda e, psA=psA: e.copy(out=MHA[:, :], in_=psA[0:64, 0:512]))
                p.s(lambda e, psB=psB: e.copy(out=MHB[:, :], in_=psB[0:64, 0:512]))
                B.reserved.discard(biA)
                B.reserved.discard(biB)
                B.done(biA, "scalar")
                B.done(biB, "scalar")
                p.wait("tensor", eS)
                for tloc in range(4):
                    ti = tb * 4 + tloc
                    for hh, MH in ((0, MHA), (1, MHB)):
                        bi, bk = B.take()
                        p.t(lambda e, tloc=tloc, MH=MH, bk=bk: e.matmul(
                            bk[0:128, 0:64], MH[:, tloc * 128:(tloc + 1) * 128],
                            IDT[0:64, 0:64], start=True, stop=True))
                        p.wait("vector", eT)
                        p.v(lambda e, ti=ti, hh=hh, bk=bk, T=subtract_into: e.tensor_tensor(
                            out=T[:, ti, hh * 64:(hh + 1) * 64],
                            in0=T[:, ti, hh * 64:(hh + 1) * 64],
                            in1=bk[0:128, 0:64], op=ALU.subtract))
                        B.done(bi, "vector")

        for pp in range(8):
            h2 = 2 * pp
            p.wait("sync", eT)
            p.wait("sync", eV)
            for kk in range(DC):
                p.dma(WPR[:, kk, 0:128],
                      dr["lb_wproj"][kk * 128:(kk + 1) * 128, h2 * 64:h2 * 64 + 128], sem=sLB)
                p.dma(WPR[:, kk, 128:256],
                      dr["lb_wproj"][kk * 128:(kk + 1) * 128, 1024 + h2 * 64:1024 + h2 * 64 + 128], sem=sLB)
                p.dma(WPR[:, kk, 256:384],
                      dr["lb_wproj"][kk * 128:(kk + 1) * 128, 2048 + h2 * 64:2048 + h2 * 64 + 128], sem=sLB)
            p.wait("tensor", sLB)
            for (dst_t, wo, bo_) in ((QTb, 0, h2 * 64), (KTb, 128, 1024 + h2 * 64)):
                for tb in range(4):
                    bi, bk = B.take()
                    for kk in range(DC):
                        p.t(lambda e, kk=kk, tb=tb, wo=wo, bk=bk: e.matmul(
                            bk[:, 0:512], WPR[:, kk, wo:wo + 128],
                            XLT[:, kk, tb * 512:(tb + 1) * 512],
                            start=(kk == 0), stop=False))
                    p.t(lambda e, tb=tb, bk=bk, bo_=bo_: e.matmul(
                        bk[:, 0:512], BPR[0:1, bo_:bo_ + 128], ONES[0:1, 0:512],
                        start=False, stop=True))
                    if tb % 2 == 0:
                        p.wait("scalar", eT)
                        p.s(lambda e, tb=tb, bk=bk, dst_t=dst_t: e.copy(
                            out=dst_t[:, tb * 512:(tb + 1) * 512], in_=bk[:, 0:512]))
                        B.done(bi, "scalar")
                    else:
                        p.wait("vector", eT)
                        p.v(lambda e, tb=tb, bk=bk, dst_t=dst_t: e.tensor_copy(
                            out=dst_t[:, tb * 512:(tb + 1) * 512], in_=bk[:, 0:512]))
                        B.done(bi, "vector")
            for ti in range(TT):
                bi, bk = B.take()
                for kk in range(DC):
                    p.t(lambda e, kk=kk, ti=ti, bk=bk: e.matmul(
                        bk[:, 0:128], XLT[:, kk, ti * 128:(ti + 1) * 128],
                        WPR[:, kk, 256:384], start=(kk == 0), stop=False))
                p.t(lambda e, bk=bk, h2=h2: e.matmul(
                    bk[:, 0:128], ONES[0:1, 0:128],
                    BPR[0:1, 2048 + h2 * 64:2048 + h2 * 64 + 128],
                    start=False, stop=True))
                p.wait("vector", eT)
                p.v(lambda e, ti=ti, bk=bk: e.tensor_copy(out=VTOK[:, ti, :], in_=bk[:, 0:128]))
                B.done(bi, "vector")
            for (SRC_T, DST) in ((KTb, ZTOK), (QTb, Q0T)):
                for ti in range(TT):
                    bi, bk = B.take()
                    p.t(lambda e, ti=ti, bk=bk, SRC_T=SRC_T: e.matmul(
                        bk[:, 0:64], SRC_T[0:64, ti * 128:(ti + 1) * 128], W0D[0:64, :],
                        start=True, stop=True))
                    bi2, bk2 = B.take()
                    p.t(lambda e, ti=ti, bk2=bk2, SRC_T=SRC_T: e.matmul(
                        bk2[:, 0:64], SRC_T[64:128, ti * 128:(ti + 1) * 128], W0D[64:128, :],
                        start=True, stop=True, tile_position=(64, 0)))
                    p.wait("vector", eT)
                    p.v(lambda e, ti=ti, bk=bk, DST=DST: e.tensor_copy(
                        out=DST[:, ti, 0:64], in_=bk[:, 0:64]))
                    p.v(lambda e, ti=ti, bk2=bk2, DST=DST: e.tensor_copy(
                        out=DST[:, ti, 64:128], in_=bk2[:, 0:64]))
                    B.done(bi, "vector")
                    B.done(bi2, "vector")
            p.wait("vector", eT)
            ln_vjp(ZTOK, z4, DZ1, dz4)
            p.v(lambda e: e.tensor_copy(out=Z2T[:, :, :], in_=ZTOK[:, :, :]))
            strips(KTb, DZ1, Z2T)
            p.wait("vector", eT)
            ln_vjp(Z2T, z24, SW, sw4)
            p.v(lambda e: e.tensor_tensor(out=DZ1[:, :, :], in0=DZ1[:, :, :],
                                          in1=SW[:, :, :], op=ALU.add))
            strips(QTb, DZ1, Q0T)
            q04 = Q0T.ap().rearrange("p a (h q) -> p a h q", q=64)
            p.wait("vector", eT)
            p.v(lambda e, q04=q04: e.tensor_reduce(out=STT[:, 0:32], in_=q04, axis=AXX,
                                                   op=ALU.add))
            stt_a = bass.AP(STT.ap().tensor, 0, [[pst, 128], [1, 2], [2, 16]])
            p.v(lambda e, stt_a=stt_a: e.tensor_reduce(out=STT[:, 32:34], in_=stt_a,
                                                       axis=AXX, op=ALU.add))
            p.v(lambda e, q04=q04: e.tensor_tensor(out=sh4, in0=q04, in1=q04, op=ALU.mult))
            p.v(lambda e: e.tensor_reduce(out=STT[:, 0:32], in_=sh4, axis=AXX, op=ALU.add))
            p.v(lambda e, stt_a=stt_a: e.tensor_reduce(out=STT[:, 34:36], in_=stt_a,
                                                       axis=AXX, op=ALU.add))
            p.wait("tensor", eV)
            bi, bk = B.take()
            p.t(lambda e, bk=bk: e.matmul(bk[0:2, 0:1], STT[:, 32:34], ONES[:, 0:1],
                                          start=True, stop=True))
            bi2, bk2 = B.take()
            p.t(lambda e, bk2=bk2: e.matmul(bk2[0:2, 0:1], STT[:, 34:36], ONES[:, 0:1],
                                            start=True, stop=True))
            p.wait("vector", eT)
            p.v(lambda e, bk=bk: e.tensor_copy(out=STT[0:2, 50:51], in_=bk[0:2, 0:1]))
            p.v(lambda e, bk2=bk2: e.tensor_copy(out=STT[0:2, 51:52], in_=bk2[0:2, 0:1]))
            B.done(bi, "vector")
            B.done(bi2, "vector")
            p.wait("sync", eV)
            p.dma(SC[0:1, 40:42], STT[0:2, 50:51], sem=sLB)
            p.dma(SC[0:1, 42:44], STT[0:2, 51:52], sem=sLB)
            p.wait("tensor", sLB)
            bi, bk = B.take()
            p.t(lambda e, bk=bk: e.matmul(bk[0:128, 0:4], ONES[0:1, 0:128], SC[0:1, 40:44],
                                          start=True, stop=True))
            p.wait("vector", eT)
            p.v(lambda e, bk=bk: e.tensor_scalar(out=STT[:, 36:40], in0=bk[0:128, 0:4],
                                                 scalar1=1.0 / float(NT * 64), scalar2=None,
                                                 op0=ALU.mult))
            B.done(bi, "vector")
            p.v(lambda e: e.tensor_tensor(out=STT[:, 40:42], in0=STT[:, 36:38],
                                          in1=STT[:, 36:38], op=ALU.mult))
            p.v(lambda e: e.tensor_tensor(out=STT[:, 42:44], in0=STT[:, 38:40],
                                          in1=STT[:, 40:42], op=ALU.subtract))
            p.v(lambda e: e.tensor_scalar(out=STT[:, 42:44], in0=STT[:, 42:44],
                                          scalar1=EPS, scalar2=None, op0=ALU.add))
            rstd_newton(STT[:, 42:44], STT[:, 44:46], STT[:, 46:48], STT[:, 48:50])
            mu3 = bass.AP(STT.ap().tensor, 36, [[pst, 128], [0, 16], [1, 2], [0, 64]])
            r3 = bass.AP(STT.ap().tensor, 44, [[pst, 128], [0, 16], [1, 2], [0, 64]])
            gstride = GNR.ap().ap[1][0]
            gng = bass.AP(GNR.ap().tensor, (h2 * 64) * gstride,
                          [[GNR.ap().ap[0][0], 128], [0, 16], [gstride, 128]])
            gnb = bass.AP(GNR.ap().tensor, (h2 * 64) * gstride + 1,
                          [[GNR.ap().ap[0][0], 128], [0, 16], [gstride, 128]])
            p.v(lambda e, q04=q04, mu3=mu3: e.tensor_tensor(out=q04, in0=q04, in1=mu3,
                                                            op=ALU.subtract))
            p.v(lambda e, q04=q04, r3=r3: e.tensor_tensor(out=q04, in0=q04, in1=r3,
                                                          op=ALU.mult))
            p.v(lambda e, gng=gng: e.tensor_tensor(out=Q0T[:, :, :], in0=Q0T[:, :, :],
                                                   in1=gng, op=ALU.mult))
            p.v(lambda e, gnb=gnb: e.tensor_tensor(out=Q0T[:, :, :], in0=Q0T[:, :, :],
                                                   in1=gnb, op=ALU.add))
            p.wait("tensor", eV)
            for ti in range(TT):
                bi, bk = B.take()
                p.t(lambda e, ti=ti, bk=bk: e.transpose(
                    bk[:, 0:128], Q0T[:, ti, :], IDT[:]))
                if ti % 2 == 0:
                    p.wait("scalar", eT)
                    p.s(lambda e, ti=ti, bk=bk: e.copy(
                        out=KTb[:, ti * 128:(ti + 1) * 128], in_=bk[:, 0:128]))
                    B.done(bi, "scalar")
                else:
                    p.wait("vector", eT)
                    p.v(lambda e, ti=ti, bk=bk: e.tensor_copy(
                        out=KTb[:, ti * 128:(ti + 1) * 128], in_=bk[:, 0:128]))
                    B.done(bi, "vector")
            p.wait("sync", eS)
            p.wait("sync", eV)
            p.dma(dr["attdr"][pp * 128:(pp + 1) * 128, :], KTb[:, :], sem=sST)
            p.wait("vector", sST)
            p.wait("scalar", sST)
        p.barrier()
        ls.close()

        ls = contextlib.ExitStack()
        WOB = p.sbuf_in(ls, "lb_WOB", (128, 8192))
        BOR = p.sbuf_in(ls, "lb_BOR", (1, D))
        sAL2 = p.sem("sAL2")
        for ppc in range(8):
            p.dma(XLT[:, ppc, :], dr["attdr"][ppc * 128:(ppc + 1) * 128, :], sem=sAL2)
        for kk in range(DC):
            p.dma(WOB[:, kk * D:(kk + 1) * D], dr["lb_wout"][kk * 128:(kk + 1) * 128, :])
        p.dma(BOR[:], dr["lb_bout"][:])
        p.wait("tensor", sAL2)
        p.wait("tensor", sDMA)
        for i in range(TT):
            p.wait("sync", sST)
            p.dma(XT[:, 1, :], src[i * 128:(i + 1) * 128, :], sem=sXD)
            p.wait("vector", sXD)
            for fb in range(2):
                bi, bk = B.take()
                for kk in range(DC):
                    p.t(lambda e, kk=kk, i=i, fb=fb, bk=bk: e.matmul(
                        bk[:, 0:512], XLT[:, kk, i * 128:(i + 1) * 128],
                        WOB[:, kk * D + fb * 512: kk * D + fb * 512 + 512],
                        start=(kk == 0), stop=False))
                p.t(lambda e, fb=fb, bk=bk: e.matmul(
                    bk[:, 0:512], ONES[0:1, 0:128], BOR[0:1, fb * 512:(fb + 1) * 512],
                    start=False, stop=True))
                p.wait("vector", eT)
                p.v(lambda e, fb=fb, bk=bk: e.tensor_tensor(
                    out=XT[:, 1, fb * 512:(fb + 1) * 512],
                    in0=XT[:, 1, fb * 512:(fb + 1) * 512],
                    in1=bk[:, 0:512], op=ALU.add))
                B.done(bi, "vector")
            p.wait("sync", eV)
            p.dma(dst[i * 128:(i + 1) * 128, :], XT[:, 1, :], sem=sST)
        p.barrier()
        ls.close()

    import functools
    sub_emitters = [l0_attn_a, l0_ffn, l0_attn_b, l0_ffn, l1_attn_a,
                    functools.partial(l1_ffn, first=True), l1_attn_b,
                    functools.partial(l1_ffn, first=False)]

    for k in range(n_sub):
        src = xd[k]
        dst = xd[k + 1]
        se = sub_emitters[k]
        se_f = se.func if isinstance(se, functools.partial) else se
        ln_into_xlt(src, tok=(se_f in (l0_ffn, l1_ffn)))
        if DBG_PHASE == f"ln{k}":
            dbg_dump_xlt(f"ln{k}")
        sub_emitters[k](src, dst)

    p.wait("sync", sST)

    with nc.Block() as block:
        p.emit(block)
    p.es.close()
    nc.finalize()
    return nc


N_SUB = 8


def kernel(**inputs):
    nc = build_program(n_sub=N_SUB)
    in_maps = [build_host_inputs(inputs, c) for c in range(NCORES)]
    res = run_bass_kernel_spmd(nc, in_maps, core_ids=list(range(NCORES)))
    out = np.stack([res.results[c]["out"] for c in range(NCORES)], axis=0)
    return out.astype(np.float32)
